# version 08
# baseline (speedup 1.0000x reference)
import numpy as np
import concourse.bass as bass
import concourse.mybir as mybir
from concourse.bass_utils import run_bass_kernel_spmd

F32 = mybir.dt.float32
BF16 = mybir.dt.bfloat16
AF = mybir.ActivationFunctionType
AX = mybir.AxisListType
ALU = mybir.AluOpType

D = 2048
H = 16
E = 128
I = 5632
NK = 2048
TQ = 1024
DC = D // 128
IB = I // 128
INV_SCALE = float(1.0 / np.sqrt(128.0))
NEG_BIG = -100000.0
LN_EPS = 1e-5
ARENA_COLS = 54272
DEBUG_B3B_BARRIER = False


class Emitter:
    STREAMS = ("pe", "dve", "act", "pool", "sp")
    SEMS = ("sdc", "sdx0", "sdx1", "sdr", "sdy", "gdb", "gdw",
            "gdf0", "gdf1", "gdd0", "gdd1", "gdo0", "gdo1", "gdt",
            "pe", "dv", "ac", "po")

    def __init__(self):
        self.s = {k: [] for k in self.STREAMS}
        self.cnt = {}
        self.handles = {}

    def sem(self, name, handle):
        self.handles[name] = handle
        self.cnt[name] = 0

    def wait(self, stream, sem, val):
        if val is None or val <= 0:
            return
        h = self.handles[sem]
        self.s[stream].append(lambda e, h=h, v=val: e.wait_ge(h, v))

    def op(self, stream, sem, build, n=1):
        h = self.handles[sem]
        self.s[stream].append(lambda e, b=build, h=h, n=n: b(e).then_inc(h, n))
        self.cnt[sem] += n
        return self.cnt[sem]

    def opq(self, stream, build):
        self.s[stream].append(build)

    def barrier(self):
        snap = dict(self.cnt)
        pairs = {"pe": "pe", "dve": "dv", "act": "ac", "pool": "po", "sp": None}
        for stream in self.STREAMS:
            for sem in ("pe", "dv", "ac", "po"):
                if pairs.get(stream) == sem:
                    continue
                self.wait(stream, sem, snap[sem])


def build_nc(taps=(), upto="b3"):
    nc = bass.Bass("TRN2", target_bir_lowering=False)

    xk = nc.dram_tensor("xk", [NK, D], F32, kind="ExternalInput").ap()
    wq = nc.dram_tensor("wq", [H, D, E], F32, kind="ExternalInput").ap()
    wk = nc.dram_tensor("wk", [H, D, E], F32, kind="ExternalInput").ap()
    wv = nc.dram_tensor("wv", [H, D, E], F32, kind="ExternalInput").ap()
    wo = nc.dram_tensor("wo", [H, E, D], F32, kind="ExternalInput").ap()
    bqt = nc.dram_tensor("bqt", [E, H], F32, kind="ExternalInput").ap()
    bkt = nc.dram_tensor("bkt", [E, H], F32, kind="ExternalInput").ap()
    bvrow = nc.dram_tensor("bvrow", [1, H * E], F32, kind="ExternalInput").ap()
    bot = nc.dram_tensor("bot", [128, DC], F32, kind="ExternalInput").ap()
    gwt = nc.dram_tensor("gwt", [D, I], F32, kind="ExternalInput").ap()
    uwt = nc.dram_tensor("uwt", [D, I], F32, kind="ExternalInput").ap()
    dwt = nc.dram_tensor("dwt", [I, D], F32, kind="ExternalInput").ap()
    gst = nc.dram_tensor("gst", [128, IB], F32, kind="ExternalInput").ap()
    ust = nc.dram_tensor("ust", [128, IB], F32, kind="ExternalInput").ap()
    dst = nc.dram_tensor("dst", [128, DC], F32, kind="ExternalInput").ap()
    ident = nc.dram_tensor("ident", [128, 128], BF16, kind="ExternalInput").ap()
    maskp = nc.dram_tensor("maskp", [128, 256], F32, kind="ExternalInput").ap()
    mask01 = nc.dram_tensor("mask01", [128, 256], BF16, kind="ExternalInput").ap()

    y = nc.dram_tensor("y", [TQ, D], F32, kind="ExternalOutput").ap()
    tap_shapes = {"xnT": [128, DC * NK], "kt": [128, 2 * NK], "qt": [128, 2 * TQ],
                  "v2": [128, 16 * 2 * 132], "zT": [128, H * TQ],
                  "hdn": [128, IB * 512], "gt": [128, DC * 1024]}
    tap_aps = {t: nc.dram_tensor("tap_" + t, tap_shapes[t], F32, kind="ExternalOutput").ap()
               for t in taps}

    em = Emitter()

    from contextlib import ExitStack
    with ExitStack() as ctx:
        def sb(name, shape, dtype):
            return ctx.enter_context(nc.sbuf_tensor(name, shape, dtype))

        def psum(name, shape, dtype):
            return ctx.enter_context(nc.psum_tensor(name, shape, dtype))

        for s in Emitter.SEMS:
            em.sem(s, ctx.enter_context(nc.semaphore(s)))

        xnT = sb("xnT", [128, DC * NK], BF16)
        zT = sb("zT", [128, H * TQ], BF16)
        id_sb = sb("id_sb", [128, 128], BF16)
        mask01_sb = sb("mask01_sb", [128, 256], BF16)
        bqt_sb = sb("bqt_sb", [128, H], F32)
        bkt_sb = sb("bkt_sb", [128, H], F32)
        bot_sb = sb("bot_sb", [128, DC], F32)
        gst_sb = sb("gst_sb", [128, IB], F32)
        ust_sb = sb("ust_sb", [128, IB], F32)
        dst_sb = sb("dst_sb", [128, DC], F32)
        bv_bf = sb("bv_bf", [1, H * E], BF16)
        ones_sb = sb("ones_sb", [1, 128], BF16)
        negm = sb("negm", [128, 8], F32)
        mx_sb = sb("mx_sb", [128, 8], F32)
        rec = sb("rec", [128, 8], F32)
        big = sb("big", [128, ARENA_COLS], BF16)

        class Carver:
            def __init__(self):
                self.off = 0

            def take(self, cols_bf16, dtype=BF16):
                o = self.off
                n = (cols_bf16 + 63) & ~63
                self.off += n
                assert self.off <= ARENA_COLS, f"arena overflow {self.off}"
                ap = big[:, o: o + cols_bf16]
                if dtype is not BF16:
                    ap = ap.bitcast(dtype)
                return ap

        ps_s = psum("ps_s", [128, 2048], F32)
        ps_ta = psum("ps_ta", [128, 1024], BF16)
        ps_tb = psum("ps_tb", [128, 1024], BF16)
        ps_z = psum("ps_z", [128, 512], F32)
        ps_w2 = psum("ps_w2", [128, 512], F32)

        def own_rhs(dc, j512):
            a = xnT.ap().rearrange("p (dc blk two w) -> p dc blk two w",
                                   dc=DC, two=2, w=64)
            return a[:, dc, 8 * j512: 8 * (j512 + 1), 0, :]

        def tap(name, src_ap, wait_stream, wait_val, cols):
            if name not in tap_aps:
                return
            em.wait("pool", wait_stream, wait_val)
            em.op("pool", "gdt", lambda e, o=tap_aps[name], s=src_ap:
                  e.dma_start(out=o, in_=s), 16)

        cv = Carver()
        xst = [cv.take(4096, F32) for _ in range(2)]
        xn_tok = [cv.take(2048) for _ in range(2)]
        stats = cv.take(64, F32)
        bn6 = cv.take(96, F32)

        nconst = 0
        for dst_t, src in [(id_sb, ident), (mask01_sb, mask01), (bqt_sb, bqt),
                           (bkt_sb, bkt), (bot_sb, bot), (gst_sb, gst),
                           (ust_sb, ust), (dst_sb, dst)]:
            em.op("sp", "sdc", lambda e, d=dst_t, s=src: e.dma_start(out=d[:, :], in_=s), 16)
            nconst += 1
        consts_sd = 16 * nconst
        em.op("pool", "gdb", lambda e: e.dma_start(out=bv_bf[:, :], in_=bvrow), 16)
        ones_po = em.op("pool", "po", lambda e: e.memset(ones_sb[:, :], 1.0))

        em.wait("dve", "sdc", consts_sd)
        bqt_scaled = em.op("dve", "dv", lambda e: e.tensor_scalar_mul(
            bqt_sb[:, :], bqt_sb[:, :], INV_SCALE))

        ev_bn = [0] * 16
        ev_xn = [0] * 16
        ev_tr = [[0, 0] for _ in range(16)]
        ev_ev = [[0, 0] for _ in range(16)]

        for i in range(16):
            buf = i % 2
            slot = i % 2
            if i >= 2:
                em.wait("sp", "dv", ev_bn[i - 2])
                em.wait("sp", "ac", ev_xn[i - 2])
            xsem = f"sdx{i % 2}"
            xdma = em.op("sp", xsem, lambda e, b=buf, i=i: e.dma_start(
                out=xst[b], in_=xk[bass.ts(i, 128), :]), 16)

            em.wait("dve", xsem, xdma)
            if i >= 2:
                em.wait("dve", "ac", ev_xn[i - 2])
            for k in range(4):
                fn = lambda e, b=buf, s=slot, k=k: e.bn_stats(
                    bn6[:, s * 24 + 6 * k: s * 24 + 6 * (k + 1)],
                    xst[b][:, 512 * k: 512 * (k + 1)])
                if k == 3:
                    ev_bn[i] = em.op("dve", "dv", fn)
                else:
                    em.opq("dve", fn)
            em.wait("dve", "dv", ev_bn[i])
            aggr = em.op("dve", "dv", lambda e, s=slot: e.bn_aggr(
                stats[:, s * 16: s * 16 + 2], bn6[:, s * 24: s * 24 + 24]))
            em.wait("dve", "dv", aggr)
            vep = em.op("dve", "dv", lambda e, s=slot: e.tensor_scalar(
                stats[:, s * 16 + 5: s * 16 + 6], stats[:, s * 16 + 1: s * 16 + 2],
                LN_EPS, None, op0=ALU.add))
            em.wait("act", "dv", vep)
            sq = em.op("act", "ac", lambda e, s=slot: e.activation(
                stats[:, s * 16 + 2: s * 16 + 3], stats[:, s * 16 + 5: s * 16 + 6],
                AF.Sqrt))
            em.wait("dve", "ac", sq)
            rst = em.op("dve", "dv", lambda e, s=slot: e.reciprocal(
                stats[:, s * 16 + 3: s * 16 + 4], stats[:, s * 16 + 2: s * 16 + 3]))
            em.wait("dve", "dv", rst)
            nmr = em.op("dve", "dv", lambda e, s=slot: e.tensor_scalar(
                stats[:, s * 16 + 4: s * 16 + 5], stats[:, s * 16: s * 16 + 1],
                stats[:, s * 16 + 3: s * 16 + 4], -1.0, op0=ALU.mult, op1=ALU.mult))
            em.wait("act", "dv", nmr)
            if i >= 2:
                em.wait("act", "pe", ev_tr[i - 2][1])
            ev_xn[i] = em.op("act", "ac", lambda e, b=buf, s=slot: e.activation(
                xn_tok[b], xst[b], AF.Identity,
                bias=stats[:, s * 16 + 4: s * 16 + 5],
                scale=stats[:, s * 16 + 3: s * 16 + 4]))

            em.wait("pe", "sdc", consts_sd)
            em.wait("pe", "ac", ev_xn[i])
            for half, bank in ((0, ps_ta), (1, ps_tb)):
                if i >= 1:
                    em.wait("pe", "dv", ev_ev[i - 1][half])
                for j8 in range(8):
                    j = half * 8 + j8
                    fn = lambda e, b=buf, bank=bank, j8=j8, j=j: e.transpose(
                        bank[:, 128 * j8: 128 * (j8 + 1)],
                        xn_tok[b][:, 128 * j: 128 * (j + 1)], id_sb[:, :])
                    if j8 == 7:
                        ev_tr[i][half] = em.op("pe", "pe", fn)
                    else:
                        em.opq("pe", fn)
                em.wait("dve", "pe", ev_tr[i][half])
                dst_ap = xnT.ap().rearrange("p (dc t) -> p dc t", t=NK)[
                    :, half * 8:(half + 1) * 8, bass.ts(i, 128)]
                src_ap = bank.ap().rearrange("p (dc t) -> p dc t", t=128)
                ev_ev[i][half] = em.op("dve", "dv", lambda e, d=dst_ap, s=src_ap:
                                       e.tensor_copy(d, s))

        xnT_dv = em.cnt["dv"]
        tap("xnT", xnT[:, :], "dv", xnT_dv, DC * NK)

        if upto == "b1":
            _finalize_partial(em, y, xst[0])
            _emit_blocks(nc, em)
            return nc

        em.barrier()
        cv = Carver()
        wk_st = cv.take(2 * DC * E)
        wq_st = cv.take(2 * DC * E)
        wv_st = cv.take(DC * 2 * E)
        KT = cv.take(2 * NK)
        QT = cv.take(2 * TQ)
        V2 = cv.take(16 * 2 * 132)
        E_sb = [cv.take(2048) for _ in range(2)]
        ET_sb = [cv.take(2048) for _ in range(2)]
        z_sb = [cv.take(128) for _ in range(2)]

        em.wait("pool", "po", ones_po)
        v2_ones = em.op("pool", "po", lambda e: e.memset(V2, 1.0))

        g_w_dma = [0] * 8
        g_kt_ac = [0] * 8
        g_qt_ac = [0] * 8
        g_v_dv = [0] * 8
        g_kv_pe = [0] * 8
        it_exp_ac = {}
        it_max_dv = {}
        it_et_pe = {}
        it_av_pe = {}
        it_rec_dv = {}
        it_zev_ac = {}
        it_zt_pe = {}
        it_ztev_dv = {}
        war_tp = {0: em.cnt["dv"], 1: em.cnt["dv"]}
        tp_par = [0]
        next_tail = [0]
        next_zt = [0]
        grp_order = [(4, 0), (0, 0), (4, 1), (0, 1),
                     (5, 0), (1, 0), (5, 1), (1, 1),
                     (2, 0), (3, 0), (2, 1), (3, 1),
                     (6, 0), (6, 1), (7, 0), (7, 1)]
        grp_soff = [0, 1536, 0, 1536,
                    0, 1536, 0, 1536,
                    0, 1024, 0, 1024,
                    0, 0, 0, 0]
        grp_depth = [2, 2, 2, 2,
                     2, 2, 2, 2,
                     2, 2, 2, 2,
                     1, 1, 1, 1]
        iter_list = [(g, p, hl) for g in range(8) for (p, hl) in grp_order]
        it_soff = [grp_soff[i % 16] for i in range(128)]
        it_depth = [grp_depth[i % 16] for i in range(128)]

        def tp_batch(tiles):
            par = tp_par[0]
            tp_par[0] ^= 1
            bank = (ps_ta, ps_tb)[par]
            em.wait("pe", "dv", war_tp[par])
            pe_cnt = 0
            for ti, (src, _) in enumerate(tiles):
                fn = lambda e, s=src, bank=bank, ti=ti: e.transpose(
                    bank[:, 128 * ti: 128 * (ti + 1)], s, id_sb[:, :])
                if ti == len(tiles) - 1:
                    pe_cnt = em.op("pe", "pe", fn)
                else:
                    em.opq("pe", fn)
            em.wait("dve", "pe", pe_cnt)
            dv_cnt = 0
            for ti, (_, dstp) in enumerate(tiles):
                fn = lambda e, d=dstp, bank=bank, ti=ti: e.tensor_copy(
                    d, bank[:, 128 * ti: 128 * (ti + 1)])
                if ti == len(tiles) - 1:
                    dv_cnt = em.op("dve", "dv", fn)
                else:
                    em.opq("dve", fn)
            war_tp[par] = dv_cnt
            return pe_cnt, dv_cnt

        def w_dma_group(g):
            h0 = 2 * g
            if g >= 1:
                em.wait("pool", "pe", g_kv_pe[g - 1])
            for src, dst_t in ((wk, wk_st), (wq, wq_st)):
                for hl in range(2):
                    in_ap = src.rearrange("h (dc p) e -> p h dc e", p=128)[:, h0 + hl, :, :]
                    out_ap = dst_t.rearrange("p (hl dc e) -> p hl dc e",
                                             hl=2, dc=DC)[:, hl, :, :]
                    em.op("pool", "gdw", lambda e, o=out_ap, a=in_ap:
                          e.dma_start(out=o, in_=a), 16)
            for hl in range(2):
                in_ap = wv.rearrange("h (dc p) e -> p h dc e", p=128)[:, h0 + hl, :, :]
                out_ap = wv_st.rearrange("p (hl dc e) -> p hl dc e",
                                         hl=2, dc=DC)[:, hl, :, :]
                g_w_dma[g] = em.op("pool", "gdw", lambda e, o=out_ap, a=in_ap:
                                   e.dma_start(out=o, in_=a), 16)

        def kv_phase(g):
            h0 = 2 * g
            em.wait("pe", "gdw", g_w_dma[g])
            em.wait("pe", "dv", xnT_dv)
            gi0 = 16 * g
            if gi0 > 0:
                em.wait("pe", "ac", it_exp_ac[gi0 - 1])
                em.wait("pe", "dv", it_max_dv[gi0 - 1])
            prev_ev = {0: 0, 1: 0, 2: 0, 3: 0}
            for hl in range(2):
                for j in range(4):
                    bank = (hl * 4 + j) % 4
                    if hl * 4 + j >= 4:
                        em.wait("pe", "ac", prev_ev[bank])
                    mm = 0
                    for dc in range(DC):
                        fn = lambda e, hl=hl, j=j, dc=dc, bank=bank: e.matmul(
                            ps_s[:, 512 * bank: 512 * (bank + 1)],
                            wk_st[:, (hl * DC + dc) * E: (hl * DC + dc + 1) * E],
                            xnT[:, dc * NK + 512 * j: dc * NK + 512 * (j + 1)],
                            start=(dc == 0), stop=(dc == DC - 1))
                        if dc == DC - 1:
                            mm = em.op("pe", "pe", fn)
                        else:
                            em.opq("pe", fn)
                    em.wait("act", "pe", mm)
                    em.wait("act", "sdc", consts_sd)
                    prev_ev[bank] = em.op("act", "ac", lambda e, hl=hl, j=j, bank=bank, h0=h0:
                                          e.activation(
                        KT[:, hl * NK + 512 * j: hl * NK + 512 * (j + 1)],
                        ps_s[:, 512 * bank: 512 * (bank + 1)],
                        AF.Identity, bias=bkt_sb[:, h0 + hl: h0 + hl + 1]))
            g_kt_ac[g] = max(prev_ev.values())
            prev_q = {0: 0, 1: 0, 2: 0, 3: 0}
            for hl in range(2):
                for j in range(2):
                    bank = (hl * 2 + j) % 4
                    em.wait("pe", "ac", prev_ev[bank])
                    if False:
                        pass
                    mm = 0
                    for dc in range(DC):
                        fn = lambda e, hl=hl, j=j, dc=dc, bank=bank: e.matmul(
                            ps_s[:, 512 * bank: 512 * (bank + 1)],
                            wq_st[:, (hl * DC + dc) * E: (hl * DC + dc + 1) * E],
                            own_rhs(dc, j), start=(dc == 0), stop=(dc == DC - 1))
                        if dc == DC - 1:
                            mm = em.op("pe", "pe", fn)
                        else:
                            em.opq("pe", fn)
                    em.wait("act", "pe", mm)
                    em.wait("act", "dv", bqt_scaled)
                    prev_q[bank] = em.op("act", "ac", lambda e, hl=hl, j=j, bank=bank, h0=h0:
                                         e.activation(
                        QT[:, hl * TQ + 512 * j: hl * TQ + 512 * (j + 1)],
                        ps_s[:, 512 * bank: 512 * (bank + 1)],
                        AF.Identity, bias=bqt_sb[:, h0 + hl: h0 + hl + 1], scale=INV_SCALE))
            g_qt_ac[g] = max(v for v in prev_q.values() if v) if any(prev_q.values()) else 0
            em.wait("pe", "gdb", 16)
            em.wait("pe", "po", v2_ones)
            if gi0 > 0:
                em.wait("pe", "ac", it_zev_ac[gi0 - 1])
                em.wait("pe", "dv", it_rec_dv[gi0 - 1])
                em.wait("dve", "pe", it_av_pe[gi0 - 1])
            prev_v = {0: 0, 1: 0, 2: 0, 3: 0}
            vslots = [ps_z[:, 0:256], ps_w2[:, 0:256],
                      ps_s[:, 0:256], ps_s[:, 512:768]]
            for kt in range(16):
                pb = vslots[kt % 4]
                if kt < 4:
                    em.wait("pe", "ac", prev_q.get(kt % 4, 0) or g_qt_ac[g])
                if kt >= 4:
                    em.wait("pe", "dv", prev_v[kt % 4])
                for dc in range(DC):
                    vrhs = wv_st.rearrange("p (hl dc e) -> p hl dc e",
                                           hl=2, dc=DC)[:, :, dc, :]
                    em.opq("pe", lambda e, kt=kt, dc=dc, pb=pb, vrhs=vrhs: e.matmul(
                        pb[:, 0:256],
                        xnT[:, dc * NK + 128 * kt: dc * NK + 128 * (kt + 1)],
                        vrhs, start=(dc == 0), stop=False))
                mm = em.op("pe", "pe", lambda e, pb=pb, h0=h0: e.matmul(
                    pb, ones_sb[:, :], bv_bf[:, h0 * E: (h0 + 2) * E],
                    start=False, stop=True))
                em.wait("dve", "pe", mm)
                for hl in range(2):
                    fn = lambda e, kt=kt, hl=hl, pb=pb: e.tensor_copy(
                        V2[:, (kt * 2 + hl) * 132: (kt * 2 + hl) * 132 + 128],
                        pb[:, hl * 128: (hl + 1) * 128])
                    if hl == 1:
                        prev_v[kt % 4] = em.op("dve", "dv", fn)
                    else:
                        em.opq("dve", fn)
            g_v_dv[g] = max(prev_v.values())
            g_kv_pe[g] = em.cnt["pe"]

        def emit_tail(j):
            if j < 0 or j >= len(iter_list) or j != next_tail[0]:
                return
            next_tail[0] = j + 1
            gj, pj, hlj = iter_list[j]
            hj = 2 * gj + hlj
            n_k = 2 * pj + 2
            ebuf = j % 2
            tbuf = j % 2
            em.wait("dve", "ac", it_exp_ac[j])
            emk = em.op("dve", "dv", lambda e, eb=ebuf, n_k=n_k: e.tensor_tensor(
                E_sb[eb][:, 128 * (n_k - 2): 128 * n_k],
                E_sb[eb][:, 128 * (n_k - 2): 128 * n_k],
                mask01_sb[:, :], op=ALU.mult))
            em.wait("pe", "ac", it_exp_ac[j])
            em.wait("pe", "dv", emk)
            last_pe = last_dv = 0
            for b0 in range(0, n_k, 4):
                tiles = [(E_sb[ebuf][:, 128 * kt: 128 * (kt + 1)],
                          ET_sb[tbuf][:, 128 * kt: 128 * (kt + 1)])
                         for kt in range(b0, min(b0 + 4, n_k))]
                last_pe, last_dv = tp_batch(tiles)
            it_et_pe[j] = last_pe
            em.wait("pe", "dv", last_dv)
            em.wait("pe", "dv", g_v_dv[gj])
            if j >= 1:
                em.wait("pe", "ac", it_zev_ac[j - 1])
                em.wait("pe", "dv", it_rec_dv[j - 1])
            av = 0
            for kt in range(n_k):
                fn = lambda e, kt=kt, hlj=hlj, tb=tbuf, n_k=n_k: e.matmul(
                    ps_z[:, 0:129],
                    ET_sb[tb][:, 128 * kt: 128 * (kt + 1)],
                    V2[:, (kt * 2 + hlj) * 132: (kt * 2 + hlj) * 132 + 129],
                    start=(kt == 0), stop=(kt == n_k - 1))
                if kt == n_k - 1:
                    av = em.op("pe", "pe", fn)
                else:
                    em.opq("pe", fn)
            it_av_pe[j] = av
            em.wait("dve", "pe", av)
            rd = em.op("dve", "dv", lambda e, j=j: e.reciprocal(
                rec[:, j % 8: j % 8 + 1], ps_z[:, 128:129]))
            it_rec_dv[j] = rd
            em.wait("act", "dv", rd)
            if j >= 2:
                em.wait("act", "pe", it_zt_pe[j - 2])
            zd = em.op("act", "ac", lambda e, j=j, zb=j % 2: e.activation(
                z_sb[zb], ps_z[:, 0:128], AF.Identity,
                scale=rec[:, j % 8: j % 8 + 1]))
            it_zev_ac[j] = zd

        def emit_zt(j):
            if j < 0 or j >= len(iter_list) or j != next_zt[0]:
                return
            next_zt[0] = j + 1
            gj, pj, hlj = iter_list[j]
            hj = 2 * gj + hlj
            em.wait("pe", "ac", it_zev_ac[j])
            zt_pe, zt_dv = tp_batch([(z_sb[j % 2],
                                      zT[:, hj * TQ + 128 * pj: hj * TQ + 128 * (pj + 1)])])
            it_zt_pe[j] = zt_pe
            it_ztev_dv[j] = zt_dv

        def flash_iter(gi):
            g, p, hl = iter_list[gi]
            n_k = 2 * p + 2
            ncols = n_k * 128
            n_ch = (ncols + 511) // 512
            soff = it_soff[gi]
            d = it_depth[gi]
            if gi >= d:
                em.wait("pe", "ac", it_exp_ac[gi - d])
                em.wait("pe", "dv", it_max_dv[gi - d])
            if gi >= 1 and it_soff[gi] != it_soff[gi - 1] and gi >= 2:
                pass
            if gi % 16 in (0, 1):
                em.wait("pe", "ac", g_kt_ac[g])
                em.wait("pe", "ac", g_qt_ac[g])
                em.wait("pe", "dv", g_v_dv[g])
            sc = 0
            em.wait("dve", "sdc", consts_sd)
            if gi >= 1:
                em.wait("dve", "dv", it_max_dv[gi - 1])
            for kc in range(n_ch):
                w = min(512, ncols - 512 * kc)
                sc = em.op("pe", "pe", lambda e, p=p, hl=hl, kc=kc, w=w, soff=soff: e.matmul(
                    ps_s[:, soff + 512 * kc: soff + 512 * kc + w],
                    QT[:, hl * TQ + 128 * p: hl * TQ + 128 * (p + 1)],
                    KT[:, hl * NK + 512 * kc: hl * NK + 512 * kc + w],
                    start=True, stop=True))
                em.wait("dve", "pe", sc)
                pm = em.op("dve", "dv", lambda e, kc=kc, w=w, soff=soff: e.reduce_max(
                    mx_sb[:, kc: kc + 1], ps_s[:, soff + 512 * kc: soff + 512 * kc + w],
                    axis=AX.X))
            em.wait("dve", "dv", pm)
            md = em.op("dve", "dv", lambda e, gi=gi, n_ch=n_ch: e.reduce_max(
                negm[:, gi % 8: gi % 8 + 1], mx_sb[:, 0:n_ch],
                axis=AX.X, negate=True))
            it_max_dv[gi] = md
            em.wait("act", "dv", md)
            if gi >= 2:
                em.wait("act", "pe", it_et_pe.get(gi - 2, 0))
            ed = em.op("act", "ac", lambda e, ncols=ncols, eb=gi % 2, gi=gi, soff=soff:
                       e.activation(
                E_sb[eb][:, 0:ncols], ps_s[:, soff: soff + ncols], AF.Exp,
                bias=negm[:, gi % 8: gi % 8 + 1]))
            it_exp_ac[gi] = ed
            emit_tail(gi - 1)
            emit_zt(gi - 2)

        for g in range(8):
            w_dma_group(g)
            kv_phase(g)
            for p in range(8):
                for hl in range(2):
                    flash_iter(16 * g + p * 2 + hl)
            emit_tail(16 * g + 15)
            emit_zt(16 * g + 14)
            emit_zt(16 * g + 15)

        tap("kt", KT, "ac", g_kt_ac[7], 2 * NK)
        tap("qt", QT, "ac", g_qt_ac[7], 2 * TQ)
        tap("v2", V2, "dv", g_v_dv[7], 16 * 2 * 132)
        tap("zT", zT[:, :], "dv", em.cnt["dv"], H * TQ)

        if upto == "b2":
            _finalize_partial(em, y, big[:, 0:4096].bitcast(F32))
            _emit_blocks(nc, em)
            return nc

        em.barrier()
        cv = Carver()
        hdnT = cv.take(IB * TQ)
        dw_carve = cv.off
        gw_st = [cv.take(DC * 128) for _ in range(2)]
        uw_st = [cv.take(DC * 128) for _ in range(2)]
        sg_sb = [cv.take(512) for _ in range(2)]
        gT = xnT[:, 0: DC * TQ]

        gu_pe = {}
        sg_ac = {}
        hdn_dv = {}
        dmm_pe = {}
        dhalf_pe = {}
        gt_ac = {}
        nit = len(iter_list)

        for ib in range(IB):
            buf = ib % 2
            if (1, ib - 2) in gu_pe:
                em.wait("pool", "pe", gu_pe[(1, ib - 2)])
            in_g = gwt.rearrange("(hc p) i -> p hc i", p=128)[:, :, bass.ts(ib, 128)]
            in_u = uwt.rearrange("(hc p) i -> p hc i", p=128)[:, :, bass.ts(ib, 128)]
            fsem = f"gdf{ib % 2}"
            em.op("pool", fsem, lambda e, b=buf, a=in_g: e.dma_start(
                out=gw_st[b], in_=a), 16)
            gu_dma = em.op("pool", fsem, lambda e, b=buf, a=in_u: e.dma_start(
                out=uw_st[b], in_=a), 16)
            gbank = ib % 2
            ubank = 2 + ib % 2
            em.wait("pe", fsem, gu_dma)
            if ib == 0:
                em.wait("pe", "ac", it_exp_ac[nit - 1])
                em.wait("pe", "dv", it_max_dv[nit - 1])
            for th in range(2):
                prev = (1, ib - 2) if th == 0 else (0, ib)
                if th == 1 or ib >= 2:
                    em.wait("pe", "ac", sg_ac.get(prev, 0))
                    em.wait("pe", "dv", hdn_dv.get(prev, 0))
                gmm = umm = 0
                for hc in range(DC):
                    fn = lambda e, b=buf, hc=hc, gb=gbank, th=th: e.matmul(
                        ps_s[:, 512 * gb: 512 * gb + 512],
                        gw_st[b][:, 128 * hc: 128 * (hc + 1)],
                        own_rhs(hc, th), start=(hc == 0), stop=(hc == DC - 1))
                    if hc == DC - 1:
                        gmm = em.op("pe", "pe", fn)
                    else:
                        em.opq("pe", fn)
                for hc in range(DC):
                    fn = lambda e, b=buf, hc=hc, ub=ubank, th=th: e.matmul(
                        ps_s[:, 512 * ub: 512 * ub + 512],
                        uw_st[b][:, 128 * hc: 128 * (hc + 1)],
                        own_rhs(hc, th), start=(hc == 0), stop=(hc == DC - 1))
                    if hc == DC - 1:
                        umm = em.op("pe", "pe", fn)
                    else:
                        em.opq("pe", fn)
                gu_pe[(th, ib)] = umm
                em.wait("act", "pe", gmm)
                em.wait("act", "sdc", consts_sd)
                if prev in hdn_dv and (th == 1 or ib >= 2):
                    em.wait("act", "dv", hdn_dv[prev])
                sgd = em.op("act", "ac", lambda e, b=th, gb=gbank, ib=ib: e.activation(
                    sg_sb[b], ps_s[:, 512 * gb: 512 * gb + 512], AF.Silu,
                    scale=gst_sb[:, ib: ib + 1]))
                sg_ac[(th, ib)] = sgd
                em.wait("dve", "pe", umm)
                em.wait("dve", "ac", sgd)
                hd = em.op("dve", "dv", lambda e, b=th, ub=ubank, ib=ib, th=th:
                           e.scalar_tensor_tensor(
                    hdnT[:, 1024 * ib + 512 * th: 1024 * ib + 512 * th + 512],
                    ps_s[:, 512 * ub: 512 * ub + 512],
                    ust_sb[:, ib: ib + 1], sg_sb[b],
                    op0=ALU.mult, op1=ALU.mult))
                hdn_dv[(th, ib)] = hd

        if "hdn" in tap_aps:
            tap("hdn", hdnT[:, 0: IB * 512], "dv", em.cnt["dv"], IB * 512)

        cvd = Carver(); cvd.off = dw_carve
        dw_st = [cvd.take(22 * 128) for _ in range(2)]
        for hc in range(DC):
            for th in range(2):
                db = (ps_z, ps_w2)[th]
                if hc >= 1:
                    em.wait("pe", "ac", gt_ac[(th, hc - 1)])
                elif th == 0:
                    em.wait("pe", "ac", it_zev_ac[nit - 1])
                    em.wait("pe", "dv", it_rec_dv[nit - 1])
            dmm = {0: 0, 1: 0}
            for hf in range(2):
                hidx = 2 * hc + hf
                buf = hidx % 2
                if hidx - 2 in dhalf_pe:
                    em.wait("pool", "pe", dhalf_pe[hidx - 2])
                elif hc == 0:
                    em.wait("pool", "pe", gu_pe[(1, IB - 1)])
                in_d = dwt.rearrange("(ib p) h -> p ib h", p=128)[
                    :, 22 * hf: 22 * (hf + 1), bass.ts(hc, 128)]
                dsem = f"gdd{hidx % 2}"
                d_dma = em.op("pool", dsem, lambda e, b=buf, a=in_d: e.dma_start(
                    out=dw_st[b], in_=a), 16)
                em.wait("pe", dsem, d_dma)
                if hc == 0 and hf == 0:
                    em.wait("pe", "dv", hdn_dv[(1, IB - 1)])
                for th in range(2):
                    db = (ps_z, ps_w2)[th]
                    for i22 in range(22):
                        ib = 22 * hf + i22
                        fn = lambda e, b=buf, i22=i22, ib=ib, db=db, th=th: e.matmul(
                            db[:, 0:512], dw_st[b][:, 128 * i22: 128 * (i22 + 1)],
                            hdnT[:, 1024 * ib + 512 * th: 1024 * ib + 512 * th + 512],
                            start=(ib == 0), stop=(ib == IB - 1))
                        if i22 == 21:
                            dmm[th] = em.op("pe", "pe", fn)
                        else:
                            em.opq("pe", fn)
                dhalf_pe[hidx] = dmm[1]
            for th in range(2):
                db = (ps_z, ps_w2)[th]
                em.wait("act", "pe", dmm[th])
                gtd = em.op("act", "ac", lambda e, hc=hc, th=th, db=db: e.activation(
                    gT[:, hc * TQ + 512 * th: hc * TQ + 512 * (th + 1)], db[:, 0:512],
                    AF.Silu, scale=dst_sb[:, hc: hc + 1]))
                gt_ac[(th, hc)] = gtd

        if "gt" in tap_aps:
            tap("gt", gT, "ac", em.cnt["ac"], DC * 1024)

        em.barrier()
        cv = Carver()
        wo_st = [cv.take(H * 128) for _ in range(2)]
        sT_sb = [cv.take(512) for _ in range(2)]
        cT_sb = [cv.take(512) for _ in range(2)]
        out_sb = [cv.take(4096, F32) for _ in range(8)]

        op_mm = {}
        op_st = {}
        op_ct = {}
        last_gt_ac = gt_ac[(1, DC - 1)]
        last_ztev = it_ztev_dv[nit - 1]

        xres_last = 0
        for jj in range(8):
            em.op("sp", "sdr", lambda e, jj=jj: e.dma_start(
                out=out_sb[jj][0:64, :], in_=xk[256 * jj: 256 * jj + 64, :]), 16)
            xres_last = em.op("sp", "sdr", lambda e, jj=jj: e.dma_start(
                out=out_sb[jj][64:128, :], in_=xk[256 * jj + 128: 256 * jj + 192, :]), 16)
        cmb_dv = [0] * 8
        for dcq in range(DC):
            buf = dcq % 2
            if dcq >= 2:
                em.wait("pool", "pe", op_mm[(1, dcq - 2)])
            in_w = wo.rearrange("h e d -> e h d")[:, :, bass.ts(dcq, 128)]
            osem = f"gdo{dcq % 2}"
            wod = em.op("pool", osem, lambda e, b=buf, a=in_w: e.dma_start(
                out=wo_st[b], in_=a), 16)
            em.wait("pe", osem, wod)
            em.wait("pe", "dv", last_ztev)
            for th in range(2):
                ob = (ps_z, ps_w2)[th]
                if dcq >= 1:
                    em.wait("pe", "ac", op_st[(th, dcq - 1)])
                else:
                    em.wait("pe", "ac", gt_ac[(th, DC - 1)])
                omm = 0
                for hh in range(H):
                    fn = lambda e, b=buf, hh=hh, ob=ob, th=th: e.matmul(
                        ob[:, 0:512], wo_st[b][:, 128 * hh: 128 * (hh + 1)],
                        zT[:, hh * TQ + 512 * th: hh * TQ + 512 * (th + 1)],
                        start=(hh == 0), stop=(hh == H - 1))
                    if hh == H - 1:
                        omm = em.op("pe", "pe", fn)
                    else:
                        em.opq("pe", fn)
                op_mm[(th, dcq)] = omm
                em.wait("act", "pe", omm)
                if dcq >= 1:
                    em.wait("act", "dv", op_ct.get((th, dcq - 1), 0))
                std = em.op("act", "ac", lambda e, b=th, dcq=dcq, ob=ob: e.activation(
                    sT_sb[b], ob[:, 0:512], AF.Identity,
                    bias=bot_sb[:, dcq: dcq + 1]))
                op_st[(th, dcq)] = std
                em.wait("dve", "ac", std)
                em.wait("dve", "ac", last_gt_ac)
                ctd = em.op("dve", "dv", lambda e, b=th, dcq=dcq, th=th: e.tensor_tensor(
                    cT_sb[b], sT_sb[b],
                    gT[:, dcq * TQ + 512 * th: dcq * TQ + 512 * (th + 1)], op=ALU.mult))
                op_ct[(th, dcq)] = ctd
                em.wait("pe", "dv", ctd)
                par = tp_par[0]
                tp_par[0] ^= 1
                bank = (ps_ta, ps_tb)[par]
                em.wait("pe", "dv", war_tp[par])
                ctp = 0
                for j in range(4):
                    fn = lambda e, b=th, j=j, bank=bank: e.transpose(
                        bank[:, 128 * j: 128 * (j + 1)],
                        cT_sb[b][:, 128 * j: 128 * (j + 1)], id_sb[:, :])
                    if j == 3:
                        ctp = em.op("pe", "pe", fn)
                    else:
                        em.opq("pe", fn)
                em.wait("dve", "pe", ctp)
                if dcq == 0 and th == 0:
                    em.wait("dve", "sdr", xres_last)
                for j in range(4):
                    jj = 4 * th + j
                    fn = lambda e, j=j, jj=jj, dcq=dcq, bank=bank: e.tensor_tensor(
                        out_sb[jj][:, 128 * dcq: 128 * (dcq + 1)],
                        bank[:, 128 * j: 128 * (j + 1)],
                        out_sb[jj][:, 128 * dcq: 128 * (dcq + 1)], op=ALU.add)
                    cmb_dv[jj] = em.op("dve", "dv", fn)
                war_tp[par] = cmb_dv[4 * th + 3]
        for jj in range(8):
            em.wait("sp", "dv", cmb_dv[jj])
            em.op("sp", "sdy", lambda e, jj=jj: e.dma_start(
                out=y[bass.ts(jj, 128), :], in_=out_sb[jj][:, :]), 16)

        for s in Emitter.SEMS[:14]:
            em.wait("sp", s, em.cnt[s])
        _emit_blocks(nc, em)
    return nc


def _last(dicts):
    m = 0
    for d in dicts:
        if d:
            m = max(m, max(d.values()))
    return m


def _finalize_partial(em, y, zero):
    em.wait("pool", "dv", em.cnt["dv"])
    em.wait("pool", "ac", em.cnt["ac"])
    em.wait("pool", "pe", em.cnt["pe"])
    z = em.op("pool", "po", lambda e, zz=zero: e.memset(zz, 0.0))
    em.wait("sp", "po", z)
    for j in range(8):
        em.op("sp", "sdy", lambda e, j=j, zz=zero: e.dma_start(
            out=y[bass.ts(j, 128), :], in_=zz), 16)
    for s in Emitter.SEMS[:14]:
        em.wait("sp", s, em.cnt[s])


def _emit_blocks(nc, em):
    with nc.Block() as block:
        @block.tensor
        def _(t):
            for fn in em.s["pe"]:
                fn(t)

        @block.vector
        def _(v):
            for fn in em.s["dve"]:
                fn(v)

        @block.scalar
        def _(s):
            for fn in em.s["act"]:
                fn(s)

        @block.gpsimd
        def _(g):
            for fn in em.s["pool"]:
                fn(g)

        @block.sync
        def _(s):
            for fn in em.s["sp"]:
                fn(s)


def _host_inputs(full, core):
    import ml_dtypes
    b = core >> 1
    rho = core & 1
    d = {}
    xb = np.asarray(full["L0_attn_x"][b], np.float32)
    if rho:
        xb = xb[np.arange(NK) ^ 64]
    d["xk"] = np.ascontiguousarray(xb)
    d["wq"] = np.ascontiguousarray(np.asarray(full["L0_attn_W_Q"], np.float32))
    d["wk"] = np.ascontiguousarray(np.asarray(full["L0_attn_W_K"], np.float32))
    d["wv"] = np.ascontiguousarray(np.asarray(full["L0_attn_W_V"], np.float32))
    d["wo"] = np.ascontiguousarray(np.asarray(full["L0_attn_W_O"], np.float32))
    d["bqt"] = np.ascontiguousarray(np.asarray(full["L0_attn_b_Q"], np.float32).T)
    d["bkt"] = np.ascontiguousarray(np.asarray(full["L0_attn_b_K"], np.float32).T)
    d["bvrow"] = np.ascontiguousarray(np.asarray(full["L0_attn_b_V"], np.float32).reshape(1, H * E))
    d["bot"] = np.ascontiguousarray(np.asarray(full["L0_attn_b_O"], np.float32).reshape(DC, 128).T)
    d["gwt"] = np.ascontiguousarray(np.asarray(full["L0_ffn_gate_w"], np.float32).T)
    d["uwt"] = np.ascontiguousarray(np.asarray(full["L0_ffn_up_w"], np.float32).T)
    d["dwt"] = np.ascontiguousarray(np.asarray(full["L0_ffn_down_w"], np.float32).T)
    d["gst"] = np.ascontiguousarray(np.asarray(full["L0_ffn_gate_s"], np.float32).reshape(IB, 128).T)
    d["ust"] = np.ascontiguousarray(np.asarray(full["L0_ffn_up_s"], np.float32).reshape(IB, 128).T)
    d["dst"] = np.ascontiguousarray(np.asarray(full["L0_ffn_down_s"], np.float32).reshape(DC, 128).T)
    d["ident"] = np.eye(128, dtype=ml_dtypes.bfloat16)
    u = np.arange(128)[:, None]
    jj = np.arange(128)[None, :]
    qoff = np.where(u < 64, 64 * rho + u, 128 + 64 * rho + (u - 64)).astype(np.int64)
    koff = (jj ^ (64 * rho)).astype(np.int64)
    m1 = np.where(koff > qoff, NEG_BIG, 0.0).astype(np.float32)
    m2 = np.where(128 + koff > qoff, NEG_BIG, 0.0).astype(np.float32)
    mk = np.concatenate([m1, m2], axis=1)
    d["maskp"] = np.ascontiguousarray(mk)
    d["mask01"] = np.ascontiguousarray((mk == 0.0).astype(ml_dtypes.bfloat16))
    return d


def _perm(rho):
    r = np.arange(TQ)
    return 64 * (2 * (r // 64) + rho) + (r % 64)


_NC_CACHE = {}


def _get_nc(taps=(), upto="b3"):
    key = (tuple(taps), upto)
    if key not in _NC_CACHE:
        _NC_CACHE[key] = build_nc(taps, upto)
    return _NC_CACHE[key]


def kernel(**inputs):
    nc = _get_nc()
    in_maps = [_host_inputs(inputs, c) for c in range(8)]
    res = run_bass_kernel_spmd(nc, in_maps, core_ids=list(range(8)))
    out = np.empty((4, NK, D), np.float32)
    for c in range(8):
        b, rho = c >> 1, c & 1
        out[b, _perm(rho)] = res.results[c]["y"]
    return out


# version 10
# speedup vs baseline: 1.0003x; 1.0003x over previous
import numpy as np
import concourse.bass as bass
import concourse.mybir as mybir
from concourse.bass_utils import run_bass_kernel_spmd

F32 = mybir.dt.float32
BF16 = mybir.dt.bfloat16
AF = mybir.ActivationFunctionType
AX = mybir.AxisListType
ALU = mybir.AluOpType

D = 2048
H = 16
E = 128
I = 5632
NK = 2048
TQ = 1024
DC = D // 128
IB = I // 128
INV_SCALE = float(1.0 / np.sqrt(128.0))
NEG_BIG = -100000.0
LN_EPS = 1e-5
ARENA_COLS = 54272
DEBUG_B3B_BARRIER = False


class Emitter:
    STREAMS = ("pe", "dve", "act", "pool", "sp")
    SEMS = ("sdc", "sdx0", "sdx1", "sdr", "sdy", "gdb", "gdw",
            "gdf0", "gdf1", "gdd0", "gdd1", "gdo0", "gdo1", "gdt",
            "pe", "dv", "ac", "po")

    def __init__(self):
        self.s = {k: [] for k in self.STREAMS}
        self.cnt = {}
        self.handles = {}

    def sem(self, name, handle):
        self.handles[name] = handle
        self.cnt[name] = 0

    def wait(self, stream, sem, val):
        if val is None or val <= 0:
            return
        h = self.handles[sem]
        self.s[stream].append(lambda e, h=h, v=val: e.wait_ge(h, v))

    def op(self, stream, sem, build, n=1):
        h = self.handles[sem]
        self.s[stream].append(lambda e, b=build, h=h, n=n: b(e).then_inc(h, n))
        self.cnt[sem] += n
        return self.cnt[sem]

    def opq(self, stream, build):
        self.s[stream].append(build)

    def barrier(self):
        snap = dict(self.cnt)
        pairs = {"pe": "pe", "dve": "dv", "act": "ac", "pool": "po", "sp": None}
        for stream in self.STREAMS:
            for sem in ("pe", "dv", "ac", "po"):
                if pairs.get(stream) == sem:
                    continue
                self.wait(stream, sem, snap[sem])


def build_nc(taps=(), upto="b3"):
    nc = bass.Bass("TRN2", target_bir_lowering=False)

    xk = nc.dram_tensor("xk", [NK, D], F32, kind="ExternalInput").ap()
    wq = nc.dram_tensor("wq", [H, D, E], F32, kind="ExternalInput").ap()
    wk = nc.dram_tensor("wk", [H, D, E], F32, kind="ExternalInput").ap()
    wv = nc.dram_tensor("wv", [H, D, E], F32, kind="ExternalInput").ap()
    wo = nc.dram_tensor("wo", [H, E, D], F32, kind="ExternalInput").ap()
    bqt = nc.dram_tensor("bqt", [E, H], F32, kind="ExternalInput").ap()
    bkt = nc.dram_tensor("bkt", [E, H], F32, kind="ExternalInput").ap()
    bvrow = nc.dram_tensor("bvrow", [1, H * E], F32, kind="ExternalInput").ap()
    bot = nc.dram_tensor("bot", [128, DC], F32, kind="ExternalInput").ap()
    gwt = nc.dram_tensor("gwt", [D, I], F32, kind="ExternalInput").ap()
    uwt = nc.dram_tensor("uwt", [D, I], F32, kind="ExternalInput").ap()
    dwt = nc.dram_tensor("dwt", [I, D], F32, kind="ExternalInput").ap()
    gst = nc.dram_tensor("gst", [128, IB], F32, kind="ExternalInput").ap()
    ust = nc.dram_tensor("ust", [128, IB], F32, kind="ExternalInput").ap()
    dst = nc.dram_tensor("dst", [128, DC], F32, kind="ExternalInput").ap()
    ident = nc.dram_tensor("ident", [128, 128], BF16, kind="ExternalInput").ap()
    maskp = nc.dram_tensor("maskp", [128, 256], F32, kind="ExternalInput").ap()
    mask01 = nc.dram_tensor("mask01", [128, 256], BF16, kind="ExternalInput").ap()

    y = nc.dram_tensor("y", [TQ, D], F32, kind="ExternalOutput").ap()
    tap_shapes = {"xnT": [128, DC * NK], "kt": [128, 2 * NK], "qt": [128, 2 * TQ],
                  "v2": [128, 16 * 2 * 132], "zT": [128, H * TQ],
                  "hdn": [128, IB * 512], "gt": [128, DC * 1024]}
    tap_aps = {t: nc.dram_tensor("tap_" + t, tap_shapes[t], F32, kind="ExternalOutput").ap()
               for t in taps}

    em = Emitter()

    from contextlib import ExitStack
    with ExitStack() as ctx:
        def sb(name, shape, dtype):
            return ctx.enter_context(nc.sbuf_tensor(name, shape, dtype))

        def psum(name, shape, dtype):
            return ctx.enter_context(nc.psum_tensor(name, shape, dtype))

        for s in Emitter.SEMS:
            em.sem(s, ctx.enter_context(nc.semaphore(s)))

        xnT = sb("xnT", [128, DC * NK], BF16)
        zT = sb("zT", [128, H * TQ], BF16)
        id_sb = sb("id_sb", [128, 128], BF16)
        mask01_sb = sb("mask01_sb", [128, 256], BF16)
        bqt_sb = sb("bqt_sb", [128, H], F32)
        bkt_sb = sb("bkt_sb", [128, H], F32)
        bot_sb = sb("bot_sb", [128, DC], F32)
        gst_sb = sb("gst_sb", [128, IB], F32)
        ust_sb = sb("ust_sb", [128, IB], F32)
        dst_sb = sb("dst_sb", [128, DC], F32)
        bv_bf = sb("bv_bf", [1, H * E], BF16)
        ones_sb = sb("ones_sb", [1, 128], BF16)
        negm = sb("negm", [128, 8], F32)
        mx_sb = sb("mx_sb", [128, 8], F32)
        rec = sb("rec", [128, 8], F32)
        big = sb("big", [128, ARENA_COLS], BF16)

        class Carver:
            def __init__(self):
                self.off = 0

            def take(self, cols_bf16, dtype=BF16):
                o = self.off
                n = (cols_bf16 + 63) & ~63
                self.off += n
                assert self.off <= ARENA_COLS, f"arena overflow {self.off}"
                ap = big[:, o: o + cols_bf16]
                if dtype is not BF16:
                    ap = ap.bitcast(dtype)
                return ap

        ps_s = psum("ps_s", [128, 2048], F32)
        ps_ta = psum("ps_ta", [128, 1024], BF16)
        ps_tb = psum("ps_tb", [128, 1024], BF16)
        ps_z = psum("ps_z", [128, 512], F32)
        ps_w2 = psum("ps_w2", [128, 512], F32)

        def own_rhs(dc, j512):
            a = xnT.ap().rearrange("p (dc blk two w) -> p dc blk two w",
                                   dc=DC, two=2, w=64)
            return a[:, dc, 8 * j512: 8 * (j512 + 1), 0, :]

        def tap(name, src_ap, wait_stream, wait_val, cols):
            if name not in tap_aps:
                return
            em.wait("pool", wait_stream, wait_val)
            em.op("pool", "gdt", lambda e, o=tap_aps[name], s=src_ap:
                  e.dma_start(out=o, in_=s), 16)

        cv = Carver()
        xst = [cv.take(4096, F32) for _ in range(2)]
        xn_tok = [cv.take(2048) for _ in range(2)]
        stats = cv.take(64, F32)
        bn6 = cv.take(96, F32)

        nconst = 0
        for dst_t, src in [(id_sb, ident), (mask01_sb, mask01), (bqt_sb, bqt),
                           (bkt_sb, bkt), (bot_sb, bot), (gst_sb, gst),
                           (ust_sb, ust), (dst_sb, dst)]:
            em.op("sp", "sdc", lambda e, d=dst_t, s=src: e.dma_start(out=d[:, :], in_=s), 16)
            nconst += 1
        consts_sd = 16 * nconst
        em.op("pool", "gdb", lambda e: e.dma_start(out=bv_bf[:, :], in_=bvrow), 16)
        ones_po = em.op("pool", "po", lambda e: e.memset(ones_sb[:, :], 1.0))

        em.wait("dve", "sdc", consts_sd)
        bqt_scaled = em.op("dve", "dv", lambda e: e.tensor_scalar_mul(
            bqt_sb[:, :], bqt_sb[:, :], INV_SCALE))

        ev_bn = [0] * 16
        ev_xn = [0] * 16
        ev_tr = [[0, 0] for _ in range(16)]
        ev_ev = [[0, 0] for _ in range(16)]

        for i in range(16):
            buf = i % 2
            slot = i % 2
            if i >= 2:
                em.wait("sp", "dv", ev_bn[i - 2])
                em.wait("sp", "ac", ev_xn[i - 2])
            xsem = f"sdx{i % 2}"
            xdma = em.op("sp", xsem, lambda e, b=buf, i=i: e.dma_start(
                out=xst[b], in_=xk[bass.ts(i, 128), :]), 16)

            em.wait("dve", xsem, xdma)
            if i >= 2:
                em.wait("dve", "ac", ev_xn[i - 2])
            for k in range(4):
                fn = lambda e, b=buf, s=slot, k=k: e.bn_stats(
                    bn6[:, s * 24 + 6 * k: s * 24 + 6 * (k + 1)],
                    xst[b][:, 512 * k: 512 * (k + 1)])
                if k == 3:
                    ev_bn[i] = em.op("dve", "dv", fn)
                else:
                    em.opq("dve", fn)
            em.wait("dve", "dv", ev_bn[i])
            aggr = em.op("dve", "dv", lambda e, s=slot: e.bn_aggr(
                stats[:, s * 16: s * 16 + 2], bn6[:, s * 24: s * 24 + 24]))
            em.wait("dve", "dv", aggr)
            vep = em.op("dve", "dv", lambda e, s=slot: e.tensor_scalar(
                stats[:, s * 16 + 5: s * 16 + 6], stats[:, s * 16 + 1: s * 16 + 2],
                LN_EPS, None, op0=ALU.add))
            em.wait("act", "dv", vep)
            sq = em.op("act", "ac", lambda e, s=slot: e.activation(
                stats[:, s * 16 + 2: s * 16 + 3], stats[:, s * 16 + 5: s * 16 + 6],
                AF.Sqrt))
            em.wait("dve", "ac", sq)
            rst = em.op("dve", "dv", lambda e, s=slot: e.reciprocal(
                stats[:, s * 16 + 3: s * 16 + 4], stats[:, s * 16 + 2: s * 16 + 3]))
            em.wait("dve", "dv", rst)
            nmr = em.op("dve", "dv", lambda e, s=slot: e.tensor_scalar(
                stats[:, s * 16 + 4: s * 16 + 5], stats[:, s * 16: s * 16 + 1],
                stats[:, s * 16 + 3: s * 16 + 4], -1.0, op0=ALU.mult, op1=ALU.mult))
            em.wait("act", "dv", nmr)
            if i >= 2:
                em.wait("act", "pe", ev_tr[i - 2][1])
            ev_xn[i] = em.op("act", "ac", lambda e, b=buf, s=slot: e.activation(
                xn_tok[b], xst[b], AF.Identity,
                bias=stats[:, s * 16 + 4: s * 16 + 5],
                scale=stats[:, s * 16 + 3: s * 16 + 4]))

            em.wait("pe", "sdc", consts_sd)
            em.wait("pe", "ac", ev_xn[i])
            for half, bank in ((0, ps_ta), (1, ps_tb)):
                if i >= 1:
                    em.wait("pe", "dv", ev_ev[i - 1][half])
                for j8 in range(8):
                    j = half * 8 + j8
                    fn = lambda e, b=buf, bank=bank, j8=j8, j=j: e.transpose(
                        bank[:, 128 * j8: 128 * (j8 + 1)],
                        xn_tok[b][:, 128 * j: 128 * (j + 1)], id_sb[:, :])
                    if j8 == 7:
                        ev_tr[i][half] = em.op("pe", "pe", fn)
                    else:
                        em.opq("pe", fn)
                em.wait("dve", "pe", ev_tr[i][half])
                dst_ap = xnT.ap().rearrange("p (dc t) -> p dc t", t=NK)[
                    :, half * 8:(half + 1) * 8, bass.ts(i, 128)]
                src_ap = bank.ap().rearrange("p (dc t) -> p dc t", t=128)
                ev_ev[i][half] = em.op("dve", "dv", lambda e, d=dst_ap, s=src_ap:
                                       e.tensor_copy(d, s))

        xnT_dv = em.cnt["dv"]
        tap("xnT", xnT[:, :], "dv", xnT_dv, DC * NK)

        if upto == "b1":
            _finalize_partial(em, y, xst[0])
            _emit_blocks(nc, em)
            return nc

        em.barrier()
        cv = Carver()
        wk_st = cv.take(2 * DC * E)
        wq_st = cv.take(2 * DC * E)
        wv_st = cv.take(DC * 2 * E)
        KT = cv.take(2 * NK)
        QT = cv.take(2 * TQ)
        V2 = cv.take(16 * 2 * 132)
        E_sb = [cv.take(2048) for _ in range(4)]
        ET_sb = [cv.take(2048) for _ in range(2)]
        z_sb = [cv.take(128) for _ in range(2)]

        em.wait("pool", "po", ones_po)
        v2_ones = em.op("pool", "po", lambda e: e.memset(V2, 1.0))

        g_w_dma = [0] * 8
        g_kt_ac = [0] * 8
        g_qt_ac = [0] * 8
        g_v_dv = [0] * 8
        g_kv_pe = [0] * 8
        it_exp_ac = {}
        it_max_dv = {}
        it_et_pe = {}
        it_av_pe = {}
        it_rec_dv = {}
        it_zev_ac = {}
        it_zt_pe = {}
        it_ztev_dv = {}
        war_tp = {0: em.cnt["dv"], 1: em.cnt["dv"]}
        tp_par = [0]
        next_tail = [0]
        next_zt = [0]
        grp_order = [(4, 0), (0, 0), (4, 1), (0, 1),
                     (5, 0), (1, 0), (5, 1), (1, 1),
                     (2, 0), (3, 0), (2, 1), (3, 1),
                     (6, 0), (6, 1), (7, 0), (7, 1)]
        grp_soff = [0, 1536, 0, 1536,
                    0, 1536, 0, 1536,
                    0, 1024, 0, 1024,
                    0, 0, 0, 0]
        grp_depth = [2, 2, 2, 2,
                     2, 2, 2, 2,
                     2, 2, 2, 2,
                     1, 1, 1, 1]
        iter_list = [(g, p, hl) for g in range(8) for (p, hl) in grp_order]
        it_soff = [grp_soff[i % 16] for i in range(128)]
        it_depth = [grp_depth[i % 16] for i in range(128)]

        def tp_batch(tiles):
            par = tp_par[0]
            tp_par[0] ^= 1
            bank = (ps_ta, ps_tb)[par]
            em.wait("pe", "dv", war_tp[par])
            pe_cnt = 0
            for ti, (src, _) in enumerate(tiles):
                fn = lambda e, s=src, bank=bank, ti=ti: e.transpose(
                    bank[:, 128 * ti: 128 * (ti + 1)], s, id_sb[:, :])
                if ti == len(tiles) - 1:
                    pe_cnt = em.op("pe", "pe", fn)
                else:
                    em.opq("pe", fn)
            em.wait("dve", "pe", pe_cnt)
            dv_cnt = 0
            for ti, (_, dstp) in enumerate(tiles):
                fn = lambda e, d=dstp, bank=bank, ti=ti: e.tensor_copy(
                    d, bank[:, 128 * ti: 128 * (ti + 1)])
                if ti == len(tiles) - 1:
                    dv_cnt = em.op("dve", "dv", fn)
                else:
                    em.opq("dve", fn)
            war_tp[par] = dv_cnt
            return pe_cnt, dv_cnt

        def w_dma_group(g):
            h0 = 2 * g
            if g >= 1:
                em.wait("pool", "pe", g_kv_pe[g - 1])
            for src, dst_t in ((wk, wk_st), (wq, wq_st)):
                for hl in range(2):
                    in_ap = src.rearrange("h (dc p) e -> p h dc e", p=128)[:, h0 + hl, :, :]
                    out_ap = dst_t.rearrange("p (hl dc e) -> p hl dc e",
                                             hl=2, dc=DC)[:, hl, :, :]
                    em.op("pool", "gdw", lambda e, o=out_ap, a=in_ap:
                          e.dma_start(out=o, in_=a), 16)
            for hl in range(2):
                in_ap = wv.rearrange("h (dc p) e -> p h dc e", p=128)[:, h0 + hl, :, :]
                out_ap = wv_st.rearrange("p (hl dc e) -> p hl dc e",
                                         hl=2, dc=DC)[:, hl, :, :]
                g_w_dma[g] = em.op("pool", "gdw", lambda e, o=out_ap, a=in_ap:
                                   e.dma_start(out=o, in_=a), 16)

        def kv_phase(g):
            h0 = 2 * g
            em.wait("pe", "gdw", g_w_dma[g])
            em.wait("pe", "dv", xnT_dv)
            gi0 = 16 * g
            if gi0 > 0:
                em.wait("pe", "ac", it_exp_ac[gi0 - 1])
                em.wait("pe", "dv", it_max_dv[gi0 - 1])
            prev_ev = {0: 0, 1: 0, 2: 0, 3: 0}
            for hl in range(2):
                for j in range(4):
                    bank = (hl * 4 + j) % 4
                    if hl * 4 + j >= 4:
                        em.wait("pe", "ac", prev_ev[bank])
                    mm = 0
                    for dc in range(DC):
                        fn = lambda e, hl=hl, j=j, dc=dc, bank=bank: e.matmul(
                            ps_s[:, 512 * bank: 512 * (bank + 1)],
                            wk_st[:, (hl * DC + dc) * E: (hl * DC + dc + 1) * E],
                            xnT[:, dc * NK + 512 * j: dc * NK + 512 * (j + 1)],
                            start=(dc == 0), stop=(dc == DC - 1))
                        if dc == DC - 1:
                            mm = em.op("pe", "pe", fn)
                        else:
                            em.opq("pe", fn)
                    em.wait("act", "pe", mm)
                    em.wait("act", "sdc", consts_sd)
                    prev_ev[bank] = em.op("act", "ac", lambda e, hl=hl, j=j, bank=bank, h0=h0:
                                          e.activation(
                        KT[:, hl * NK + 512 * j: hl * NK + 512 * (j + 1)],
                        ps_s[:, 512 * bank: 512 * (bank + 1)],
                        AF.Identity, bias=bkt_sb[:, h0 + hl: h0 + hl + 1]))
            g_kt_ac[g] = max(prev_ev.values())
            prev_q = {0: 0, 1: 0, 2: 0, 3: 0}
            for hl in range(2):
                for j in range(2):
                    bank = (hl * 2 + j) % 4
                    em.wait("pe", "ac", prev_ev[bank])
                    if False:
                        pass
                    mm = 0
                    for dc in range(DC):
                        fn = lambda e, hl=hl, j=j, dc=dc, bank=bank: e.matmul(
                            ps_s[:, 512 * bank: 512 * (bank + 1)],
                            wq_st[:, (hl * DC + dc) * E: (hl * DC + dc + 1) * E],
                            own_rhs(dc, j), start=(dc == 0), stop=(dc == DC - 1))
                        if dc == DC - 1:
                            mm = em.op("pe", "pe", fn)
                        else:
                            em.opq("pe", fn)
                    em.wait("act", "pe", mm)
                    em.wait("act", "dv", bqt_scaled)
                    prev_q[bank] = em.op("act", "ac", lambda e, hl=hl, j=j, bank=bank, h0=h0:
                                         e.activation(
                        QT[:, hl * TQ + 512 * j: hl * TQ + 512 * (j + 1)],
                        ps_s[:, 512 * bank: 512 * (bank + 1)],
                        AF.Identity, bias=bqt_sb[:, h0 + hl: h0 + hl + 1], scale=INV_SCALE))
            g_qt_ac[g] = max(v for v in prev_q.values() if v) if any(prev_q.values()) else 0
            em.wait("pe", "gdb", 16)
            em.wait("pe", "po", v2_ones)
            if gi0 > 0:
                em.wait("pe", "ac", it_zev_ac[gi0 - 1])
                em.wait("pe", "dv", it_rec_dv[gi0 - 1])
                em.wait("dve", "pe", it_av_pe[gi0 - 1])
            prev_v = {0: 0, 1: 0, 2: 0, 3: 0}
            vslots = [ps_z[:, 0:256], ps_w2[:, 0:256],
                      ps_s[:, 0:256], ps_s[:, 512:768]]
            for kt in range(16):
                pb = vslots[kt % 4]
                if kt < 4:
                    em.wait("pe", "ac", prev_q.get(kt % 4, 0) or g_qt_ac[g])
                if kt >= 4:
                    em.wait("pe", "dv", prev_v[kt % 4])
                for dc in range(DC):
                    vrhs = wv_st.rearrange("p (hl dc e) -> p hl dc e",
                                           hl=2, dc=DC)[:, :, dc, :]
                    em.opq("pe", lambda e, kt=kt, dc=dc, pb=pb, vrhs=vrhs: e.matmul(
                        pb[:, 0:256],
                        xnT[:, dc * NK + 128 * kt: dc * NK + 128 * (kt + 1)],
                        vrhs, start=(dc == 0), stop=False))
                mm = em.op("pe", "pe", lambda e, pb=pb, h0=h0: e.matmul(
                    pb, ones_sb[:, :], bv_bf[:, h0 * E: (h0 + 2) * E],
                    start=False, stop=True))
                em.wait("dve", "pe", mm)
                for hl in range(2):
                    fn = lambda e, kt=kt, hl=hl, pb=pb: e.tensor_copy(
                        V2[:, (kt * 2 + hl) * 132: (kt * 2 + hl) * 132 + 128],
                        pb[:, hl * 128: (hl + 1) * 128])
                    if hl == 1:
                        prev_v[kt % 4] = em.op("dve", "dv", fn)
                    else:
                        em.opq("dve", fn)
            g_v_dv[g] = max(prev_v.values())
            g_kv_pe[g] = em.cnt["pe"]

        def emit_tail(j):
            if j < 0 or j >= len(iter_list) or j != next_tail[0]:
                return
            next_tail[0] = j + 1
            gj, pj, hlj = iter_list[j]
            hj = 2 * gj + hlj
            n_k = 2 * pj + 2
            ebuf = j % 4
            tbuf = j % 2
            em.wait("dve", "ac", it_exp_ac[j])
            emk = em.op("dve", "dv", lambda e, eb=ebuf, n_k=n_k: e.tensor_tensor(
                E_sb[eb][:, 128 * (n_k - 2): 128 * n_k],
                E_sb[eb][:, 128 * (n_k - 2): 128 * n_k],
                mask01_sb[:, :], op=ALU.mult))
            em.wait("pe", "ac", it_exp_ac[j])
            em.wait("pe", "dv", emk)
            last_pe = last_dv = 0
            for b0 in range(0, n_k, 4):
                tiles = [(E_sb[ebuf][:, 128 * kt: 128 * (kt + 1)],
                          ET_sb[tbuf][:, 128 * kt: 128 * (kt + 1)])
                         for kt in range(b0, min(b0 + 4, n_k))]
                last_pe, last_dv = tp_batch(tiles)
            it_et_pe[j] = last_pe
            em.wait("pe", "dv", last_dv)
            em.wait("pe", "dv", g_v_dv[gj])
            if j >= 1:
                em.wait("pe", "ac", it_zev_ac[j - 1])
                em.wait("pe", "dv", it_rec_dv[j - 1])
            av = 0
            for kt in range(n_k):
                fn = lambda e, kt=kt, hlj=hlj, tb=tbuf, n_k=n_k: e.matmul(
                    ps_z[:, 0:129],
                    ET_sb[tb][:, 128 * kt: 128 * (kt + 1)],
                    V2[:, (kt * 2 + hlj) * 132: (kt * 2 + hlj) * 132 + 129],
                    start=(kt == 0), stop=(kt == n_k - 1))
                if kt == n_k - 1:
                    av = em.op("pe", "pe", fn)
                else:
                    em.opq("pe", fn)
            it_av_pe[j] = av
            em.wait("dve", "pe", av)
            rd = em.op("dve", "dv", lambda e, j=j: e.reciprocal(
                rec[:, j % 8: j % 8 + 1], ps_z[:, 128:129]))
            it_rec_dv[j] = rd
            em.wait("act", "dv", rd)
            if j >= 2:
                em.wait("act", "pe", it_zt_pe[j - 2])
            zd = em.op("act", "ac", lambda e, j=j, zb=j % 2: e.activation(
                z_sb[zb], ps_z[:, 0:128], AF.Identity,
                scale=rec[:, j % 8: j % 8 + 1]))
            it_zev_ac[j] = zd

        def emit_zt(j):
            if j < 0 or j >= len(iter_list) or j != next_zt[0]:
                return
            next_zt[0] = j + 1
            gj, pj, hlj = iter_list[j]
            hj = 2 * gj + hlj
            em.wait("pe", "ac", it_zev_ac[j])
            zt_pe, zt_dv = tp_batch([(z_sb[j % 2],
                                      zT[:, hj * TQ + 128 * pj: hj * TQ + 128 * (pj + 1)])])
            it_zt_pe[j] = zt_pe
            it_ztev_dv[j] = zt_dv

        def flash_iter(gi):
            g, p, hl = iter_list[gi]
            n_k = 2 * p + 2
            ncols = n_k * 128
            n_ch = (ncols + 511) // 512
            soff = it_soff[gi]
            d = it_depth[gi]
            if gi >= d:
                em.wait("pe", "ac", it_exp_ac[gi - d])
                em.wait("pe", "dv", it_max_dv[gi - d])
            if gi >= 1 and it_soff[gi] != it_soff[gi - 1] and gi >= 2:
                pass
            if gi % 16 in (0, 1):
                em.wait("pe", "ac", g_kt_ac[g])
                em.wait("pe", "ac", g_qt_ac[g])
                em.wait("pe", "dv", g_v_dv[g])
            sc = 0
            em.wait("dve", "sdc", consts_sd)
            if gi >= 1:
                em.wait("dve", "dv", it_max_dv[gi - 1])
            for kc in range(n_ch):
                w = min(512, ncols - 512 * kc)
                sc = em.op("pe", "pe", lambda e, p=p, hl=hl, kc=kc, w=w, soff=soff: e.matmul(
                    ps_s[:, soff + 512 * kc: soff + 512 * kc + w],
                    QT[:, hl * TQ + 128 * p: hl * TQ + 128 * (p + 1)],
                    KT[:, hl * NK + 512 * kc: hl * NK + 512 * kc + w],
                    start=True, stop=True))
                em.wait("dve", "pe", sc)
                pm = em.op("dve", "dv", lambda e, kc=kc, w=w, soff=soff: e.reduce_max(
                    mx_sb[:, kc: kc + 1], ps_s[:, soff + 512 * kc: soff + 512 * kc + w],
                    axis=AX.X))
            em.wait("dve", "dv", pm)
            md = em.op("dve", "dv", lambda e, gi=gi, n_ch=n_ch: e.reduce_max(
                negm[:, gi % 8: gi % 8 + 1], mx_sb[:, 0:n_ch],
                axis=AX.X, negate=True))
            it_max_dv[gi] = md
            em.wait("act", "dv", md)
            if gi >= 4:
                em.wait("act", "pe", it_et_pe.get(gi - 4, 0))
            ed = em.op("act", "ac", lambda e, ncols=ncols, eb=gi % 4, gi=gi, soff=soff:
                       e.activation(
                E_sb[eb][:, 0:ncols], ps_s[:, soff: soff + ncols], AF.Exp,
                bias=negm[:, gi % 8: gi % 8 + 1]))
            it_exp_ac[gi] = ed
            emit_tail(gi - 2)
            emit_zt(gi - 3)

        for g in range(8):
            w_dma_group(g)
            kv_phase(g)
            for p in range(8):
                for hl in range(2):
                    flash_iter(16 * g + p * 2 + hl)
            emit_tail(16 * g + 14)
            emit_zt(16 * g + 13)
            emit_tail(16 * g + 15)
            emit_zt(16 * g + 14)
            emit_zt(16 * g + 15)

        tap("kt", KT, "ac", g_kt_ac[7], 2 * NK)
        tap("qt", QT, "ac", g_qt_ac[7], 2 * TQ)
        tap("v2", V2, "dv", g_v_dv[7], 16 * 2 * 132)
        tap("zT", zT[:, :], "dv", em.cnt["dv"], H * TQ)

        if upto == "b2":
            _finalize_partial(em, y, big[:, 0:4096].bitcast(F32))
            _emit_blocks(nc, em)
            return nc

        em.barrier()
        cv = Carver()
        hdnT = cv.take(IB * TQ)
        dw_carve = cv.off
        gw_st = [cv.take(DC * 128) for _ in range(2)]
        uw_st = [cv.take(DC * 128) for _ in range(2)]
        sg_sb = [cv.take(512) for _ in range(2)]
        gT = xnT[:, 0: DC * TQ]

        gu_pe = {}
        sg_ac = {}
        hdn_dv = {}
        dmm_pe = {}
        dhalf_pe = {}
        gt_ac = {}
        nit = len(iter_list)

        for ib in range(IB):
            buf = ib % 2
            if (1, ib - 2) in gu_pe:
                em.wait("pool", "pe", gu_pe[(1, ib - 2)])
            in_g = gwt.rearrange("(hc p) i -> p hc i", p=128)[:, :, bass.ts(ib, 128)]
            in_u = uwt.rearrange("(hc p) i -> p hc i", p=128)[:, :, bass.ts(ib, 128)]
            fsem = f"gdf{ib % 2}"
            em.op("pool", fsem, lambda e, b=buf, a=in_g: e.dma_start(
                out=gw_st[b], in_=a), 16)
            gu_dma = em.op("pool", fsem, lambda e, b=buf, a=in_u: e.dma_start(
                out=uw_st[b], in_=a), 16)
            gbank = ib % 2
            ubank = 2 + ib % 2
            em.wait("pe", fsem, gu_dma)
            if ib == 0:
                em.wait("pe", "ac", it_exp_ac[nit - 1])
                em.wait("pe", "dv", it_max_dv[nit - 1])
            for th in range(2):
                prev = (1, ib - 2) if th == 0 else (0, ib)
                if th == 1 or ib >= 2:
                    em.wait("pe", "ac", sg_ac.get(prev, 0))
                    em.wait("pe", "dv", hdn_dv.get(prev, 0))
                gmm = umm = 0
                for hc in range(DC):
                    fn = lambda e, b=buf, hc=hc, gb=gbank, th=th: e.matmul(
                        ps_s[:, 512 * gb: 512 * gb + 512],
                        gw_st[b][:, 128 * hc: 128 * (hc + 1)],
                        own_rhs(hc, th), start=(hc == 0), stop=(hc == DC - 1))
                    if hc == DC - 1:
                        gmm = em.op("pe", "pe", fn)
                    else:
                        em.opq("pe", fn)
                for hc in range(DC):
                    fn = lambda e, b=buf, hc=hc, ub=ubank, th=th: e.matmul(
                        ps_s[:, 512 * ub: 512 * ub + 512],
                        uw_st[b][:, 128 * hc: 128 * (hc + 1)],
                        own_rhs(hc, th), start=(hc == 0), stop=(hc == DC - 1))
                    if hc == DC - 1:
                        umm = em.op("pe", "pe", fn)
                    else:
                        em.opq("pe", fn)
                gu_pe[(th, ib)] = umm
                em.wait("act", "pe", gmm)
                em.wait("act", "sdc", consts_sd)
                if prev in hdn_dv and (th == 1 or ib >= 2):
                    em.wait("act", "dv", hdn_dv[prev])
                sgd = em.op("act", "ac", lambda e, b=th, gb=gbank, ib=ib: e.activation(
                    sg_sb[b], ps_s[:, 512 * gb: 512 * gb + 512], AF.Silu,
                    scale=gst_sb[:, ib: ib + 1]))
                sg_ac[(th, ib)] = sgd
                em.wait("dve", "pe", umm)
                em.wait("dve", "ac", sgd)
                hd = em.op("dve", "dv", lambda e, b=th, ub=ubank, ib=ib, th=th:
                           e.scalar_tensor_tensor(
                    hdnT[:, 1024 * ib + 512 * th: 1024 * ib + 512 * th + 512],
                    ps_s[:, 512 * ub: 512 * ub + 512],
                    ust_sb[:, ib: ib + 1], sg_sb[b],
                    op0=ALU.mult, op1=ALU.mult))
                hdn_dv[(th, ib)] = hd

        if "hdn" in tap_aps:
            tap("hdn", hdnT[:, 0: IB * 512], "dv", em.cnt["dv"], IB * 512)

        cvd = Carver(); cvd.off = dw_carve
        dw_st = [cvd.take(22 * 128) for _ in range(2)]
        for hc in range(DC):
            for th in range(2):
                db = (ps_z, ps_w2)[th]
                if hc >= 1:
                    em.wait("pe", "ac", gt_ac[(th, hc - 1)])
                elif th == 0:
                    em.wait("pe", "ac", it_zev_ac[nit - 1])
                    em.wait("pe", "dv", it_rec_dv[nit - 1])
            dmm = {0: 0, 1: 0}
            for hf in range(2):
                hidx = 2 * hc + hf
                buf = hidx % 2
                if hidx - 2 in dhalf_pe:
                    em.wait("pool", "pe", dhalf_pe[hidx - 2])
                elif hc == 0:
                    em.wait("pool", "pe", gu_pe[(1, IB - 1)])
                in_d = dwt.rearrange("(ib p) h -> p ib h", p=128)[
                    :, 22 * hf: 22 * (hf + 1), bass.ts(hc, 128)]
                dsem = f"gdd{hidx % 2}"
                d_dma = em.op("pool", dsem, lambda e, b=buf, a=in_d: e.dma_start(
                    out=dw_st[b], in_=a), 16)
                em.wait("pe", dsem, d_dma)
                if hc == 0 and hf == 0:
                    em.wait("pe", "dv", hdn_dv[(1, IB - 1)])
                for th in range(2):
                    db = (ps_z, ps_w2)[th]
                    for i22 in range(22):
                        ib = 22 * hf + i22
                        fn = lambda e, b=buf, i22=i22, ib=ib, db=db, th=th: e.matmul(
                            db[:, 0:512], dw_st[b][:, 128 * i22: 128 * (i22 + 1)],
                            hdnT[:, 1024 * ib + 512 * th: 1024 * ib + 512 * th + 512],
                            start=(ib == 0), stop=(ib == IB - 1))
                        if i22 == 21:
                            dmm[th] = em.op("pe", "pe", fn)
                        else:
                            em.opq("pe", fn)
                dhalf_pe[hidx] = dmm[1]
            for th in range(2):
                db = (ps_z, ps_w2)[th]
                em.wait("act", "pe", dmm[th])
                gtd = em.op("act", "ac", lambda e, hc=hc, th=th, db=db: e.activation(
                    gT[:, hc * TQ + 512 * th: hc * TQ + 512 * (th + 1)], db[:, 0:512],
                    AF.Silu, scale=dst_sb[:, hc: hc + 1]))
                gt_ac[(th, hc)] = gtd

        if "gt" in tap_aps:
            tap("gt", gT, "ac", em.cnt["ac"], DC * 1024)

        em.barrier()
        cv = Carver()
        wo_st = [cv.take(H * 128) for _ in range(2)]
        sT_sb = [cv.take(512) for _ in range(2)]
        cT_sb = [cv.take(512) for _ in range(2)]
        out_sb = [cv.take(4096, F32) for _ in range(8)]

        op_mm = {}
        op_st = {}
        op_ct = {}
        last_gt_ac = gt_ac[(1, DC - 1)]
        last_ztev = it_ztev_dv[nit - 1]

        xres_last = 0
        for jj in range(8):
            em.op("sp", "sdr", lambda e, jj=jj: e.dma_start(
                out=out_sb[jj][0:64, :], in_=xk[256 * jj: 256 * jj + 64, :]), 16)
            xres_last = em.op("sp", "sdr", lambda e, jj=jj: e.dma_start(
                out=out_sb[jj][64:128, :], in_=xk[256 * jj + 128: 256 * jj + 192, :]), 16)
        cmb_dv = [0] * 8
        for dcq in range(DC):
            buf = dcq % 2
            if dcq >= 2:
                em.wait("pool", "pe", op_mm[(1, dcq - 2)])
            in_w = wo.rearrange("h e d -> e h d")[:, :, bass.ts(dcq, 128)]
            osem = f"gdo{dcq % 2}"
            wod = em.op("pool", osem, lambda e, b=buf, a=in_w: e.dma_start(
                out=wo_st[b], in_=a), 16)
            em.wait("pe", osem, wod)
            em.wait("pe", "dv", last_ztev)
            for th in range(2):
                ob = (ps_z, ps_w2)[th]
                if dcq >= 1:
                    em.wait("pe", "ac", op_st[(th, dcq - 1)])
                else:
                    em.wait("pe", "ac", gt_ac[(th, DC - 1)])
                omm = 0
                for hh in range(H):
                    fn = lambda e, b=buf, hh=hh, ob=ob, th=th: e.matmul(
                        ob[:, 0:512], wo_st[b][:, 128 * hh: 128 * (hh + 1)],
                        zT[:, hh * TQ + 512 * th: hh * TQ + 512 * (th + 1)],
                        start=(hh == 0), stop=(hh == H - 1))
                    if hh == H - 1:
                        omm = em.op("pe", "pe", fn)
                    else:
                        em.opq("pe", fn)
                op_mm[(th, dcq)] = omm
                em.wait("act", "pe", omm)
                if dcq >= 1:
                    em.wait("act", "dv", op_ct.get((th, dcq - 1), 0))
                std = em.op("act", "ac", lambda e, b=th, dcq=dcq, ob=ob: e.activation(
                    sT_sb[b], ob[:, 0:512], AF.Identity,
                    bias=bot_sb[:, dcq: dcq + 1]))
                op_st[(th, dcq)] = std
                em.wait("dve", "ac", std)
                em.wait("dve", "ac", last_gt_ac)
                ctd = em.op("dve", "dv", lambda e, b=th, dcq=dcq, th=th: e.tensor_tensor(
                    cT_sb[b], sT_sb[b],
                    gT[:, dcq * TQ + 512 * th: dcq * TQ + 512 * (th + 1)], op=ALU.mult))
                op_ct[(th, dcq)] = ctd
                em.wait("pe", "dv", ctd)
                par = tp_par[0]
                tp_par[0] ^= 1
                bank = (ps_ta, ps_tb)[par]
                em.wait("pe", "dv", war_tp[par])
                ctp = 0
                for j in range(4):
                    fn = lambda e, b=th, j=j, bank=bank: e.transpose(
                        bank[:, 128 * j: 128 * (j + 1)],
                        cT_sb[b][:, 128 * j: 128 * (j + 1)], id_sb[:, :])
                    if j == 3:
                        ctp = em.op("pe", "pe", fn)
                    else:
                        em.opq("pe", fn)
                em.wait("dve", "pe", ctp)
                if dcq == 0 and th == 0:
                    em.wait("dve", "sdr", xres_last)
                for j in range(4):
                    jj = 4 * th + j
                    fn = lambda e, j=j, jj=jj, dcq=dcq, bank=bank: e.tensor_tensor(
                        out_sb[jj][:, 128 * dcq: 128 * (dcq + 1)],
                        bank[:, 128 * j: 128 * (j + 1)],
                        out_sb[jj][:, 128 * dcq: 128 * (dcq + 1)], op=ALU.add)
                    cmb_dv[jj] = em.op("dve", "dv", fn)
                war_tp[par] = cmb_dv[4 * th + 3]
        for jj in range(8):
            em.wait("sp", "dv", cmb_dv[jj])
            em.op("sp", "sdy", lambda e, jj=jj: e.dma_start(
                out=y[bass.ts(jj, 128), :], in_=out_sb[jj][:, :]), 16)

        for s in Emitter.SEMS[:14]:
            em.wait("sp", s, em.cnt[s])
        _emit_blocks(nc, em)
    return nc


def _last(dicts):
    m = 0
    for d in dicts:
        if d:
            m = max(m, max(d.values()))
    return m


def _finalize_partial(em, y, zero):
    em.wait("pool", "dv", em.cnt["dv"])
    em.wait("pool", "ac", em.cnt["ac"])
    em.wait("pool", "pe", em.cnt["pe"])
    z = em.op("pool", "po", lambda e, zz=zero: e.memset(zz, 0.0))
    em.wait("sp", "po", z)
    for j in range(8):
        em.op("sp", "sdy", lambda e, j=j, zz=zero: e.dma_start(
            out=y[bass.ts(j, 128), :], in_=zz), 16)
    for s in Emitter.SEMS[:14]:
        em.wait("sp", s, em.cnt[s])


def _emit_blocks(nc, em):
    with nc.Block() as block:
        @block.tensor
        def _(t):
            for fn in em.s["pe"]:
                fn(t)

        @block.vector
        def _(v):
            for fn in em.s["dve"]:
                fn(v)

        @block.scalar
        def _(s):
            for fn in em.s["act"]:
                fn(s)

        @block.gpsimd
        def _(g):
            for fn in em.s["pool"]:
                fn(g)

        @block.sync
        def _(s):
            for fn in em.s["sp"]:
                fn(s)


def _host_inputs(full, core):
    import ml_dtypes
    b = core >> 1
    rho = core & 1
    d = {}
    xb = np.asarray(full["L0_attn_x"][b], np.float32)
    if rho:
        xb = xb[np.arange(NK) ^ 64]
    d["xk"] = np.ascontiguousarray(xb)
    d["wq"] = np.ascontiguousarray(np.asarray(full["L0_attn_W_Q"], np.float32))
    d["wk"] = np.ascontiguousarray(np.asarray(full["L0_attn_W_K"], np.float32))
    d["wv"] = np.ascontiguousarray(np.asarray(full["L0_attn_W_V"], np.float32))
    d["wo"] = np.ascontiguousarray(np.asarray(full["L0_attn_W_O"], np.float32))
    d["bqt"] = np.ascontiguousarray(np.asarray(full["L0_attn_b_Q"], np.float32).T)
    d["bkt"] = np.ascontiguousarray(np.asarray(full["L0_attn_b_K"], np.float32).T)
    d["bvrow"] = np.ascontiguousarray(np.asarray(full["L0_attn_b_V"], np.float32).reshape(1, H * E))
    d["bot"] = np.ascontiguousarray(np.asarray(full["L0_attn_b_O"], np.float32).reshape(DC, 128).T)
    d["gwt"] = np.ascontiguousarray(np.asarray(full["L0_ffn_gate_w"], np.float32).T)
    d["uwt"] = np.ascontiguousarray(np.asarray(full["L0_ffn_up_w"], np.float32).T)
    d["dwt"] = np.ascontiguousarray(np.asarray(full["L0_ffn_down_w"], np.float32).T)
    d["gst"] = np.ascontiguousarray(np.asarray(full["L0_ffn_gate_s"], np.float32).reshape(IB, 128).T)
    d["ust"] = np.ascontiguousarray(np.asarray(full["L0_ffn_up_s"], np.float32).reshape(IB, 128).T)
    d["dst"] = np.ascontiguousarray(np.asarray(full["L0_ffn_down_s"], np.float32).reshape(DC, 128).T)
    d["ident"] = np.eye(128, dtype=ml_dtypes.bfloat16)
    u = np.arange(128)[:, None]
    jj = np.arange(128)[None, :]
    qoff = np.where(u < 64, 64 * rho + u, 128 + 64 * rho + (u - 64)).astype(np.int64)
    koff = (jj ^ (64 * rho)).astype(np.int64)
    m1 = np.where(koff > qoff, NEG_BIG, 0.0).astype(np.float32)
    m2 = np.where(128 + koff > qoff, NEG_BIG, 0.0).astype(np.float32)
    mk = np.concatenate([m1, m2], axis=1)
    d["maskp"] = np.ascontiguousarray(mk)
    d["mask01"] = np.ascontiguousarray((mk == 0.0).astype(ml_dtypes.bfloat16))
    return d


def _perm(rho):
    r = np.arange(TQ)
    return 64 * (2 * (r // 64) + rho) + (r % 64)


_NC_CACHE = {}


def _get_nc(taps=(), upto="b3"):
    key = (tuple(taps), upto)
    if key not in _NC_CACHE:
        _NC_CACHE[key] = build_nc(taps, upto)
    return _NC_CACHE[key]


def kernel(**inputs):
    nc = _get_nc()
    in_maps = [_host_inputs(inputs, c) for c in range(8)]
    res = run_bass_kernel_spmd(nc, in_maps, core_ids=list(range(8)))
    out = np.empty((4, NK, D), np.float32)
    for c in range(8):
        b, rho = c >> 1, c & 1
        out[b, _perm(rho)] = res.results[c]["y"]
    return out


# version 13
# speedup vs baseline: 1.0922x; 1.0919x over previous
import numpy as np
import concourse.bass as bass
import concourse.mybir as mybir
from concourse.bass_utils import run_bass_kernel_spmd

F32 = mybir.dt.float32
BF16 = mybir.dt.bfloat16
AF = mybir.ActivationFunctionType
AX = mybir.AxisListType
ALU = mybir.AluOpType

D = 2048
H = 16
E = 128
I = 5632
NK = 2048
TQ = 1024
DC = D // 128
IB = I // 128
INV_SCALE = float(1.0 / np.sqrt(128.0))
NEG_BIG = -100000.0
LN_EPS = 1e-5
ARENA_COLS = 54272
DEBUG_B3B_BARRIER = False


class Emitter:
    STREAMS = ("pe", "dve", "act", "pool", "sp")
    SEMS = ("sdc", "sdx0", "sdx1", "sdr", "sdy", "gdb", "gdw",
            "gdf0", "gdf1", "gdd0", "gdd1", "gdo0", "gdo1", "gdt",
            "pe", "dv", "ac", "po")

    def __init__(self):
        self.s = {k: [] for k in self.STREAMS}
        self.cnt = {}
        self.handles = {}

    def sem(self, name, handle):
        self.handles[name] = handle
        self.cnt[name] = 0

    def wait(self, stream, sem, val):
        if val is None or val <= 0:
            return
        h = self.handles[sem]
        self.s[stream].append(lambda e, h=h, v=val: e.wait_ge(h, v))

    def op(self, stream, sem, build, n=1):
        h = self.handles[sem]
        self.s[stream].append(lambda e, b=build, h=h, n=n: b(e).then_inc(h, n))
        self.cnt[sem] += n
        return self.cnt[sem]

    def opq(self, stream, build):
        self.s[stream].append(build)

    def barrier(self):
        snap = dict(self.cnt)
        pairs = {"pe": "pe", "dve": "dv", "act": "ac", "pool": "po", "sp": None}
        for stream in self.STREAMS:
            for sem in ("pe", "dv", "ac", "po"):
                if pairs.get(stream) == sem:
                    continue
                self.wait(stream, sem, snap[sem])


def build_nc(taps=(), upto="b3"):
    nc = bass.Bass("TRN2", target_bir_lowering=False)

    xk = nc.dram_tensor("xk", [NK, D], F32, kind="ExternalInput").ap()
    wq = nc.dram_tensor("wq", [H, D, E], F32, kind="ExternalInput").ap()
    wk = nc.dram_tensor("wk", [H, D, E], F32, kind="ExternalInput").ap()
    wv = nc.dram_tensor("wv", [H, D, E], F32, kind="ExternalInput").ap()
    wo = nc.dram_tensor("wo", [H, E, D], F32, kind="ExternalInput").ap()
    bqt = nc.dram_tensor("bqt", [E, H], F32, kind="ExternalInput").ap()
    bkt = nc.dram_tensor("bkt", [E, H], F32, kind="ExternalInput").ap()
    bvrow = nc.dram_tensor("bvrow", [1, H * E], F32, kind="ExternalInput").ap()
    bot = nc.dram_tensor("bot", [128, DC], F32, kind="ExternalInput").ap()
    gwt = nc.dram_tensor("gwt", [D, I], F32, kind="ExternalInput").ap()
    uwt = nc.dram_tensor("uwt", [D, I], F32, kind="ExternalInput").ap()
    dwt = nc.dram_tensor("dwt", [I, D], F32, kind="ExternalInput").ap()
    gst = nc.dram_tensor("gst", [128, IB], F32, kind="ExternalInput").ap()
    ust = nc.dram_tensor("ust", [128, IB], F32, kind="ExternalInput").ap()
    dst = nc.dram_tensor("dst", [128, DC], F32, kind="ExternalInput").ap()
    ident = nc.dram_tensor("ident", [128, 128], BF16, kind="ExternalInput").ap()
    maskp = nc.dram_tensor("maskp", [128, 256], F32, kind="ExternalInput").ap()
    mask01 = nc.dram_tensor("mask01", [128, 256], BF16, kind="ExternalInput").ap()

    y = nc.dram_tensor("y", [TQ, D], F32, kind="ExternalOutput").ap()
    tap_shapes = {"xnT": [128, DC * NK], "kt": [128, 2 * NK], "qt": [128, 2 * TQ],
                  "v2": [128, 16 * 2 * 132], "zT": [128, H * TQ],
                  "hdn": [128, IB * 512], "gt": [128, DC * 1024]}
    tap_aps = {t: nc.dram_tensor("tap_" + t, tap_shapes[t], F32, kind="ExternalOutput").ap()
               for t in taps}

    em = Emitter()

    from contextlib import ExitStack
    with ExitStack() as ctx:
        def sb(name, shape, dtype):
            return ctx.enter_context(nc.sbuf_tensor(name, shape, dtype))

        def psum(name, shape, dtype):
            return ctx.enter_context(nc.psum_tensor(name, shape, dtype))

        for s in Emitter.SEMS:
            em.sem(s, ctx.enter_context(nc.semaphore(s)))

        xnT = sb("xnT", [128, DC * NK], BF16)
        zT = sb("zT", [128, H * TQ], BF16)
        id_sb = sb("id_sb", [128, 128], BF16)
        mask01_sb = sb("mask01_sb", [128, 256], BF16)
        bqt_sb = sb("bqt_sb", [128, H], F32)
        bkt_sb = sb("bkt_sb", [128, H], F32)
        bot_sb = sb("bot_sb", [128, DC], F32)
        gst_sb = sb("gst_sb", [128, IB], F32)
        ust_sb = sb("ust_sb", [128, IB], F32)
        dst_sb = sb("dst_sb", [128, DC], F32)
        bv_bf = sb("bv_bf", [1, H * E], BF16)
        ones_sb = sb("ones_sb", [1, 128], BF16)
        negm = sb("negm", [128, 8], F32)
        mx_sb = sb("mx_sb", [128, 8], F32)
        rec = sb("rec", [128, 8], F32)
        big = sb("big", [128, ARENA_COLS], BF16)

        class Carver:
            def __init__(self):
                self.off = 0

            def take(self, cols_bf16, dtype=BF16):
                o = self.off
                n = (cols_bf16 + 63) & ~63
                self.off += n
                assert self.off <= ARENA_COLS, f"arena overflow {self.off}"
                ap = big[:, o: o + cols_bf16]
                if dtype is not BF16:
                    ap = ap.bitcast(dtype)
                return ap

        ps_s = psum("ps_s", [128, 2048], F32)
        ps_ta = psum("ps_ta", [128, 1024], BF16)
        ps_tb = psum("ps_tb", [128, 1024], BF16)
        ps_z = psum("ps_z", [128, 512], F32)
        ps_w2 = psum("ps_w2", [128, 512], F32)

        def own_rhs(dc, j512):
            a = xnT.ap().rearrange("p (dc blk two w) -> p dc blk two w",
                                   dc=DC, two=2, w=64)
            return a[:, dc, 8 * j512: 8 * (j512 + 1), 0, :]

        def tap(name, src_ap, wait_stream, wait_val, cols):
            if name not in tap_aps:
                return
            em.wait("pool", wait_stream, wait_val)
            em.op("pool", "gdt", lambda e, o=tap_aps[name], s=src_ap:
                  e.dma_start(out=o, in_=s), 16)

        cv = Carver()
        xst = [cv.take(4096, F32) for _ in range(4)]
        xn_tok = [cv.take(2048) for _ in range(4)]
        stats = cv.take(128, F32)
        bn6 = cv.take(192, F32)

        nconst = 0
        for dst_t, src in [(id_sb, ident), (mask01_sb, mask01), (bqt_sb, bqt),
                           (bkt_sb, bkt), (bot_sb, bot), (gst_sb, gst),
                           (ust_sb, ust), (dst_sb, dst)]:
            em.op("sp", "sdc", lambda e, d=dst_t, s=src: e.dma_start(out=d[:, :], in_=s), 16)
            nconst += 1
        consts_sd = 16 * nconst
        em.op("pool", "gdb", lambda e: e.dma_start(out=bv_bf[:, :], in_=bvrow), 16)
        ones_po = em.op("pool", "po", lambda e: e.memset(ones_sb[:, :], 1.0))

        em.wait("dve", "sdc", consts_sd)
        bqt_scaled = em.op("dve", "dv", lambda e: e.tensor_scalar_mul(
            bqt_sb[:, :], bqt_sb[:, :], INV_SCALE))

        ev_bn = [0] * 16
        ev_xn = [0] * 16
        xsem_done = {}
        ev_tr = [[0, 0] for _ in range(16)]
        ev_ev = [[0, 0] for _ in range(16)]

        for i in range(16):
            buf = i % 4
            slot = i % 4
            if i >= 4:
                em.wait("sp", "dv", ev_bn[i - 4])
                em.wait("sp", "ac", ev_xn[i - 4])
            if i >= 2:
                em.wait("sp", xsem_done[i - 2][0], xsem_done[i - 2][1])
            xsem = f"sdx{i % 2}"
            xdma = em.op("sp", xsem, lambda e, b=buf, i=i: e.dma_start(
                out=xst[b], in_=xk[bass.ts(i, 128), :]), 16)
            xsem_done[i] = (xsem, xdma)

            em.wait("dve", xsem, xdma)
            if i >= 4:
                em.wait("dve", "ac", ev_xn[i - 4])
            for k in range(4):
                fn = lambda e, b=buf, s=slot, k=k: e.bn_stats(
                    bn6[:, s * 24 + 6 * k: s * 24 + 6 * (k + 1)],
                    xst[b][:, 512 * k: 512 * (k + 1)])
                if k == 3:
                    ev_bn[i] = em.op("dve", "dv", fn)
                else:
                    em.opq("dve", fn)
            em.wait("dve", "dv", ev_bn[i])
            aggr = em.op("dve", "dv", lambda e, s=slot: e.bn_aggr(
                stats[:, s * 16: s * 16 + 2], bn6[:, s * 24: s * 24 + 24]))
            em.wait("dve", "dv", aggr)
            vep = em.op("dve", "dv", lambda e, s=slot: e.tensor_scalar(
                stats[:, s * 16 + 5: s * 16 + 6], stats[:, s * 16 + 1: s * 16 + 2],
                LN_EPS, None, op0=ALU.add))
            em.wait("act", "dv", vep)
            sq = em.op("act", "ac", lambda e, s=slot: e.activation(
                stats[:, s * 16 + 2: s * 16 + 3], stats[:, s * 16 + 5: s * 16 + 6],
                AF.Sqrt))
            em.wait("dve", "ac", sq)
            rst = em.op("dve", "dv", lambda e, s=slot: e.reciprocal(
                stats[:, s * 16 + 3: s * 16 + 4], stats[:, s * 16 + 2: s * 16 + 3]))
            em.wait("dve", "dv", rst)
            nmr = em.op("dve", "dv", lambda e, s=slot: e.tensor_scalar(
                stats[:, s * 16 + 4: s * 16 + 5], stats[:, s * 16: s * 16 + 1],
                stats[:, s * 16 + 3: s * 16 + 4], -1.0, op0=ALU.mult, op1=ALU.mult))
            em.wait("act", "dv", nmr)
            if i >= 4:
                em.wait("act", "pe", ev_tr[i - 4][1])
            ev_xn[i] = em.op("act", "ac", lambda e, b=buf, s=slot: e.activation(
                xn_tok[b], xst[b], AF.Identity,
                bias=stats[:, s * 16 + 4: s * 16 + 5],
                scale=stats[:, s * 16 + 3: s * 16 + 4]))

            em.wait("pe", "sdc", consts_sd)
            em.wait("pe", "ac", ev_xn[i])
            for half, bank in ((0, ps_ta), (1, ps_tb)):
                if i >= 1:
                    em.wait("pe", "dv", ev_ev[i - 1][half])
                for j8 in range(8):
                    j = half * 8 + j8
                    fn = lambda e, b=buf, bank=bank, j8=j8, j=j: e.transpose(
                        bank[:, 128 * j8: 128 * (j8 + 1)],
                        xn_tok[b][:, 128 * j: 128 * (j + 1)], id_sb[:, :])
                    if j8 == 7:
                        ev_tr[i][half] = em.op("pe", "pe", fn)
                    else:
                        em.opq("pe", fn)
                em.wait("dve", "pe", ev_tr[i][half])
                dst_ap = xnT.ap().rearrange("p (dc t) -> p dc t", t=NK)[
                    :, half * 8:(half + 1) * 8, bass.ts(i, 128)]
                src_ap = bank.ap().rearrange("p (dc t) -> p dc t", t=128)
                ev_ev[i][half] = em.op("dve", "dv", lambda e, d=dst_ap, s=src_ap:
                                       e.tensor_copy(d, s))

        xnT_dv = em.cnt["dv"]
        tap("xnT", xnT[:, :], "dv", xnT_dv, DC * NK)

        if upto == "b1":
            _finalize_partial(em, y, xst[0])
            _emit_blocks(nc, em)
            return nc

        em.barrier()
        cv = Carver()
        wk_st = cv.take(2 * DC * E)
        wq_st = cv.take(2 * DC * E)
        wv_st = cv.take(DC * 2 * E)
        KT = cv.take(2 * NK)
        QT = cv.take(2 * TQ)
        V2 = cv.take(16 * 2 * 132)
        sqtmp = cv.take(4096)
        kn_bf = cv.take(64)
        knT = cv.take(256)
        kmx1 = cv.take(64)
        kmaxneg = cv.take(4, F32)
        qn_sb = cv.take(32, F32)
        negmh = cv.take(32, F32)
        ones_col = cv.take(64)
        E_sb = [cv.take(2048) for _ in range(2)]
        ET_sb = [cv.take(2048) for _ in range(2)]
        z_sb = [cv.take(128) for _ in range(2)]

        em.wait("pool", "po", ones_po)
        em.op("pool", "po", lambda e: e.memset(ones_col[:, 0:1], 1.0))
        v2_ones = em.op("pool", "po", lambda e: e.memset(V2, 1.0))

        g_w_dma = [0] * 8
        g_kt_ac = [0] * 8
        g_qt_ac = [0] * 8
        g_v_dv = [0] * 8
        g_negmh_dv = [0] * 8
        g_kv_pe = [0] * 8
        it_exp_ac = {}
        it_max_dv = {}
        it_et_pe = {}
        it_av_pe = {}
        it_rec_dv = {}
        it_zev_ac = {}
        it_zt_pe = {}
        it_ztev_dv = {}
        war_tp = {0: em.cnt["dv"], 1: em.cnt["dv"]}
        tp_par = [0]
        next_tail = [0]
        next_zt = [0]
        iter_list = [(g, p, hl) for g in range(8) for p in range(8) for hl in range(2)]

        def tp_batch(tiles):
            par = tp_par[0]
            tp_par[0] ^= 1
            bank = (ps_ta, ps_tb)[par]
            em.wait("pe", "dv", war_tp[par])
            pe_cnt = 0
            for ti, (src, _) in enumerate(tiles):
                fn = lambda e, s=src, bank=bank, ti=ti: e.transpose(
                    bank[:, 128 * ti: 128 * (ti + 1)], s, id_sb[:, :])
                if ti == len(tiles) - 1:
                    pe_cnt = em.op("pe", "pe", fn)
                else:
                    em.opq("pe", fn)
            em.wait("dve", "pe", pe_cnt)
            dv_cnt = 0
            for ti, (_, dstp) in enumerate(tiles):
                fn = lambda e, d=dstp, bank=bank, ti=ti: e.tensor_copy(
                    d, bank[:, 128 * ti: 128 * (ti + 1)])
                if ti == len(tiles) - 1:
                    dv_cnt = em.op("dve", "dv", fn)
                else:
                    em.opq("dve", fn)
            war_tp[par] = dv_cnt
            return pe_cnt, dv_cnt

        def w_dma_group(g):
            h0 = 2 * g
            if g >= 1:
                em.wait("pool", "pe", g_kv_pe[g - 1])
            for src, dst_t in ((wk, wk_st), (wq, wq_st)):
                for hl in range(2):
                    in_ap = src.rearrange("h (dc p) e -> p h dc e", p=128)[:, h0 + hl, :, :]
                    out_ap = dst_t.rearrange("p (hl dc e) -> p hl dc e",
                                             hl=2, dc=DC)[:, hl, :, :]
                    em.op("pool", "gdw", lambda e, o=out_ap, a=in_ap:
                          e.dma_start(out=o, in_=a), 16)
            for hl in range(2):
                in_ap = wv.rearrange("h (dc p) e -> p h dc e", p=128)[:, h0 + hl, :, :]
                out_ap = wv_st.rearrange("p (hl dc e) -> p hl dc e",
                                         hl=2, dc=DC)[:, hl, :, :]
                g_w_dma[g] = em.op("pool", "gdw", lambda e, o=out_ap, a=in_ap:
                                   e.dma_start(out=o, in_=a), 16)

        def kv_phase(g):
            h0 = 2 * g
            em.wait("pe", "gdw", g_w_dma[g])
            em.wait("pe", "dv", xnT_dv)
            gi0 = 16 * g
            if gi0 > 0:
                em.wait("pe", "ac", it_exp_ac[gi0 - 1])
            prev_ev = {0: 0, 1: 0, 2: 0, 3: 0}
            for hl in range(2):
                for j in range(4):
                    bank = (hl * 4 + j) % 4
                    if hl * 4 + j >= 4:
                        em.wait("pe", "ac", prev_ev[bank])
                    mm = 0
                    for dc in range(DC):
                        fn = lambda e, hl=hl, j=j, dc=dc, bank=bank: e.matmul(
                            ps_s[:, 512 * bank: 512 * (bank + 1)],
                            wk_st[:, (hl * DC + dc) * E: (hl * DC + dc + 1) * E],
                            xnT[:, dc * NK + 512 * j: dc * NK + 512 * (j + 1)],
                            start=(dc == 0), stop=(dc == DC - 1))
                        if dc == DC - 1:
                            mm = em.op("pe", "pe", fn)
                        else:
                            em.opq("pe", fn)
                    em.wait("act", "pe", mm)
                    em.wait("act", "sdc", consts_sd)
                    prev_ev[bank] = em.op("act", "ac", lambda e, hl=hl, j=j, bank=bank, h0=h0:
                                          e.activation(
                        KT[:, hl * NK + 512 * j: hl * NK + 512 * (j + 1)],
                        ps_s[:, 512 * bank: 512 * (bank + 1)],
                        AF.Identity, bias=bkt_sb[:, h0 + hl: h0 + hl + 1]))
            g_kt_ac[g] = max(prev_ev.values())
            prev_q = {0: 0, 1: 0, 2: 0, 3: 0}
            for hl in range(2):
                for j in range(2):
                    bank = (hl * 2 + j) % 4
                    em.wait("pe", "ac", prev_ev[bank])
                    if False:
                        pass
                    mm = 0
                    for dc in range(DC):
                        fn = lambda e, hl=hl, j=j, dc=dc, bank=bank: e.matmul(
                            ps_s[:, 512 * bank: 512 * (bank + 1)],
                            wq_st[:, (hl * DC + dc) * E: (hl * DC + dc + 1) * E],
                            own_rhs(dc, j), start=(dc == 0), stop=(dc == DC - 1))
                        if dc == DC - 1:
                            mm = em.op("pe", "pe", fn)
                        else:
                            em.opq("pe", fn)
                    em.wait("act", "pe", mm)
                    em.wait("act", "dv", bqt_scaled)
                    prev_q[bank] = em.op("act", "ac", lambda e, hl=hl, j=j, bank=bank, h0=h0:
                                         e.activation(
                        QT[:, hl * TQ + 512 * j: hl * TQ + 512 * (j + 1)],
                        ps_s[:, 512 * bank: 512 * (bank + 1)],
                        AF.Identity, bias=bqt_sb[:, h0 + hl: h0 + hl + 1], scale=INV_SCALE))
            g_qt_ac[g] = max(v for v in prev_q.values() if v) if any(prev_q.values()) else 0
            em.wait("pe", "gdb", 16)
            em.wait("pe", "po", v2_ones)
            if gi0 > 0:
                em.wait("pe", "ac", it_zev_ac[gi0 - 1])
                em.wait("pe", "dv", it_rec_dv[gi0 - 1])
                em.wait("dve", "pe", it_av_pe[gi0 - 1])
            prev_v = {0: 0, 1: 0, 2: 0, 3: 0}
            vslots = [ps_z[:, 0:256], ps_w2[:, 0:256],
                      ps_s[:, 0:256], ps_s[:, 512:768]]
            for kt in range(16):
                pb = vslots[kt % 4]
                if kt < 4:
                    em.wait("pe", "ac", prev_q.get(kt % 4, 0) or g_qt_ac[g])
                if kt >= 4:
                    em.wait("pe", "dv", prev_v[kt % 4])
                for dc in range(DC):
                    vrhs = wv_st.rearrange("p (hl dc e) -> p hl dc e",
                                           hl=2, dc=DC)[:, :, dc, :]
                    em.opq("pe", lambda e, kt=kt, dc=dc, pb=pb, vrhs=vrhs: e.matmul(
                        pb[:, 0:256],
                        xnT[:, dc * NK + 128 * kt: dc * NK + 128 * (kt + 1)],
                        vrhs, start=(dc == 0), stop=False))
                mm = em.op("pe", "pe", lambda e, pb=pb, h0=h0: e.matmul(
                    pb, ones_sb[:, :], bv_bf[:, h0 * E: (h0 + 2) * E],
                    start=False, stop=True))
                em.wait("dve", "pe", mm)
                for hl in range(2):
                    fn = lambda e, kt=kt, hl=hl, pb=pb: e.tensor_copy(
                        V2[:, (kt * 2 + hl) * 132: (kt * 2 + hl) * 132 + 128],
                        pb[:, hl * 128: (hl + 1) * 128])
                    if hl == 1:
                        prev_v[kt % 4] = em.op("dve", "dv", fn)
                    else:
                        em.opq("dve", fn)
            g_v_dv[g] = max(prev_v.values())

            em.wait("act", "ac", g_kt_ac[g])
            if g >= 1:
                em.wait("act", "ac", it_exp_ac[16 * g - 1])
            k2 = em.op("act", "ac", lambda e: e.activation(
                sqtmp[:, 0:4096], KT[:, 0:4096], AF.Square))
            em.wait("pe", "ac", k2)
            em.wait("pe", "po", v2_ones)
            nmm = 0
            for hl in range(2):
                for kt in range(16):
                    fn = lambda e, hl=hl, kt=kt: e.matmul(
                        ps_w2[:, hl * 16 + kt: hl * 16 + kt + 1],
                        sqtmp[:, hl * 2048 + 128 * kt: hl * 2048 + 128 * (kt + 1)],
                        ones_col[:, 0:1], start=True, stop=True)
                    nmm = em.op("pe", "pe", fn) if (hl, kt) == (1, 15) else em.opq("pe", fn)
            em.wait("act", "pe", nmm)
            ksq = em.op("act", "ac", lambda e: e.activation(
                kn_bf[:, 0:32], ps_w2[:, 0:32], AF.Sqrt))
            em.wait("pe", "ac", ksq)
            em.wait("pe", "dv", war_tp[0])
            t1 = em.op("pe", "pe", lambda e: e.transpose(
                ps_ta[:, 0:128].bitcast(BF16)[0:32, 0:128] if False else ps_ta[0:32, 0:128],
                kn_bf[:, 0:32], id_sb[:, :]))
            em.wait("dve", "pe", t1)
            e1 = em.op("dve", "dv", lambda e: e.tensor_copy(knT[0:32, 0:128], ps_ta[0:32, 0:128]))
            war_tp[0] = e1
            em.wait("dve", "dv", e1)
            r1 = em.op("dve", "dv", lambda e: e.reduce_max(
                knT[0:32, 128:129].bitcast(F32)[0:32, 0:1] if False else knT[0:32, 128:129],
                knT[0:32, 0:128], axis=AX.X))
            em.wait("pe", "dv", r1)
            em.wait("pe", "dv", war_tp[1])
            t2 = em.op("pe", "pe", lambda e: e.transpose(
                ps_tb[0:1, 0:32], knT[0:32, 128:129], id_sb[0:32, 0:32]))
            em.wait("dve", "pe", t2)
            e2 = em.op("dve", "dv", lambda e: e.tensor_copy(kmx1[0:1, 0:32], ps_tb[0:1, 0:32]))
            war_tp[1] = e2
            em.wait("dve", "dv", e2)
            r2a = em.op("dve", "dv", lambda e: e.reduce_max(
                kmx1[0:1, 32:33], kmx1[0:1, 0:16], axis=AX.X))
            r2b = em.op("dve", "dv", lambda e: e.reduce_max(
                kmx1[0:1, 33:34], kmx1[0:1, 16:32], axis=AX.X))
            em.wait("pe", "dv", r2b)
            bmm = 0
            for hl in range(2):
                fn = lambda e, hl=hl: e.matmul(
                    ps_w2[:, 64 + hl: 64 + hl + 1], ones_sb[:, :],
                    kmx1[0:1, 32 + hl: 33 + hl], start=True, stop=True)
                bmm = em.op("pe", "pe", fn) if hl == 1 else em.opq("pe", fn)
            em.wait("act", "pe", bmm)
            kneg = em.op("act", "ac", lambda e: e.activation(
                kmaxneg[:, 0:2], ps_w2[:, 64:66], AF.Identity, scale=-1.0))
            em.wait("act", "ac", g_qt_ac[g])
            q2 = em.op("act", "ac", lambda e: e.activation(
                sqtmp[:, 0:2048], QT[:, 0:2048], AF.Square))
            em.wait("pe", "ac", q2)
            qmm = 0
            for hl in range(2):
                for p8 in range(8):
                    fn = lambda e, hl=hl, p8=p8: e.matmul(
                        ps_w2[:, 96 + hl * 8 + p8: 97 + hl * 8 + p8],
                        sqtmp[:, hl * 1024 + 128 * p8: hl * 1024 + 128 * (p8 + 1)],
                        ones_col[:, 0:1], start=True, stop=True)
                    qmm = em.op("pe", "pe", fn) if (hl, p8) == (1, 7) else em.opq("pe", fn)
            em.wait("act", "pe", qmm)
            qsq = em.op("act", "ac", lambda e: e.activation(
                qn_sb[:, 0:16], ps_w2[:, 96:112], AF.Sqrt))
            em.wait("dve", "ac", max(qsq, kneg))
            nh = 0
            for hl in range(2):
                nh = em.op("dve", "dv", lambda e, hl=hl: e.tensor_scalar(
                    negmh[:, hl * 8: hl * 8 + 8], qn_sb[:, hl * 8: hl * 8 + 8],
                    kmaxneg[:, hl: hl + 1], -75.0, op0=ALU.mult, op1=ALU.max))
            g_negmh_dv[g] = nh
            g_kv_pe[g] = em.cnt["pe"]

        def emit_tail(j):
            if j < 0 or j >= len(iter_list) or j != next_tail[0]:
                return
            next_tail[0] = j + 1
            gj, pj, hlj = iter_list[j]
            hj = 2 * gj + hlj
            n_k = 2 * pj + 2
            ebuf = j % 2
            tbuf = j % 2
            em.wait("dve", "ac", it_exp_ac[j])
            emk = em.op("dve", "dv", lambda e, eb=ebuf, n_k=n_k: e.tensor_tensor(
                E_sb[eb][:, 128 * (n_k - 2): 128 * n_k],
                E_sb[eb][:, 128 * (n_k - 2): 128 * n_k],
                mask01_sb[:, :], op=ALU.mult))
            em.wait("pe", "ac", it_exp_ac[j])
            em.wait("pe", "dv", emk)
            last_pe = last_dv = 0
            for b0 in range(0, n_k, 4):
                tiles = [(E_sb[ebuf][:, 128 * kt: 128 * (kt + 1)],
                          ET_sb[tbuf][:, 128 * kt: 128 * (kt + 1)])
                         for kt in range(b0, min(b0 + 4, n_k))]
                last_pe, last_dv = tp_batch(tiles)
            it_et_pe[j] = last_pe
            em.wait("pe", "dv", last_dv)
            em.wait("pe", "dv", g_v_dv[gj])
            if j >= 1:
                em.wait("pe", "ac", it_zev_ac[j - 1])
                em.wait("pe", "dv", it_rec_dv[j - 1])
            av = 0
            for kt in range(n_k):
                fn = lambda e, kt=kt, hlj=hlj, tb=tbuf, n_k=n_k: e.matmul(
                    ps_z[:, 0:129],
                    ET_sb[tb][:, 128 * kt: 128 * (kt + 1)],
                    V2[:, (kt * 2 + hlj) * 132: (kt * 2 + hlj) * 132 + 129],
                    start=(kt == 0), stop=(kt == n_k - 1))
                if kt == n_k - 1:
                    av = em.op("pe", "pe", fn)
                else:
                    em.opq("pe", fn)
            it_av_pe[j] = av
            em.wait("dve", "pe", av)
            rd = em.op("dve", "dv", lambda e, j=j: e.reciprocal(
                rec[:, j % 8: j % 8 + 1], ps_z[:, 128:129]))
            it_rec_dv[j] = rd
            em.wait("act", "dv", rd)
            if j >= 2:
                em.wait("act", "pe", it_zt_pe[j - 2])
            zd = em.op("act", "ac", lambda e, j=j, zb=j % 2: e.activation(
                z_sb[zb], ps_z[:, 0:128], AF.Identity,
                scale=rec[:, j % 8: j % 8 + 1]))
            it_zev_ac[j] = zd

        def emit_zt(j):
            if j < 0 or j >= len(iter_list) or j != next_zt[0]:
                return
            next_zt[0] = j + 1
            gj, pj, hlj = iter_list[j]
            hj = 2 * gj + hlj
            em.wait("pe", "ac", it_zev_ac[j])
            zt_pe, zt_dv = tp_batch([(z_sb[j % 2],
                                      zT[:, hj * TQ + 128 * pj: hj * TQ + 128 * (pj + 1)])])
            it_zt_pe[j] = zt_pe
            it_ztev_dv[j] = zt_dv

        def flash_iter(gi):
            g, p, hl = iter_list[gi]
            n_k = 2 * p + 2
            ncols = n_k * 128
            n_ch = (ncols + 511) // 512
            soff = (gi % 2) * 1024 if p <= 3 else 0
            if p >= 4 and gi >= 1:
                em.wait("pe", "ac", it_exp_ac[gi - 1])
                em.wait("pe", "dv", it_max_dv[gi - 1])
            elif gi >= 2:
                em.wait("pe", "ac", it_exp_ac[gi - 2])
                em.wait("pe", "dv", it_max_dv[gi - 2])
            if gi % 16 in (0, 1):
                em.wait("pe", "ac", g_kt_ac[g])
                em.wait("pe", "ac", g_qt_ac[g])
                em.wait("pe", "dv", g_v_dv[g])
            sc = 0
            for kc in range(n_ch):
                w = min(512, ncols - 512 * kc)
                fn = lambda e, p=p, hl=hl, kc=kc, w=w, soff=soff: e.matmul(
                    ps_s[:, soff + 512 * kc: soff + 512 * kc + w],
                    QT[:, hl * TQ + 128 * p: hl * TQ + 128 * (p + 1)],
                    KT[:, hl * NK + 512 * kc: hl * NK + 512 * kc + w],
                    start=True, stop=True)
                if kc == n_ch - 1:
                    sc = em.op("pe", "pe", fn)
                else:
                    em.opq("pe", fn)
            it_max_dv[gi] = 0
            em.wait("act", "pe", sc)
            em.wait("act", "dv", g_negmh_dv[g])
            if gi >= 2:
                em.wait("act", "pe", it_et_pe.get(gi - 2, 0))
            ed = em.op("act", "ac", lambda e, ncols=ncols, eb=gi % 2, p=p, hl=hl, soff=soff:
                       e.activation(
                E_sb[eb][:, 0:ncols], ps_s[:, soff: soff + ncols], AF.Exp,
                bias=negmh[:, hl * 8 + p: hl * 8 + p + 1]))
            it_exp_ac[gi] = ed
            emit_tail(gi - 1)
            emit_zt(gi - 2)

        for g in range(8):
            w_dma_group(g)
            kv_phase(g)
            for p in range(8):
                for hl in range(2):
                    flash_iter(16 * g + p * 2 + hl)
            emit_tail(16 * g + 15)
            emit_zt(16 * g + 14)
            emit_zt(16 * g + 15)

        tap("kt", KT, "ac", g_kt_ac[7], 2 * NK)
        tap("qt", QT, "ac", g_qt_ac[7], 2 * TQ)
        tap("v2", V2, "dv", g_v_dv[7], 16 * 2 * 132)
        tap("zT", zT[:, :], "dv", em.cnt["dv"], H * TQ)

        if upto == "b2":
            _finalize_partial(em, y, big[:, 0:4096].bitcast(F32))
            _emit_blocks(nc, em)
            return nc

        em.barrier()
        cv = Carver()
        hdnT = cv.take(IB * TQ)
        dw_carve = cv.off
        gw_st = [cv.take(DC * 128) for _ in range(2)]
        uw_st = [cv.take(DC * 128) for _ in range(2)]
        sg_sb = [cv.take(512) for _ in range(2)]
        gT = xnT[:, 0: DC * TQ]

        gu_pe = {}
        sg_ac = {}
        hdn_dv = {}
        dmm_pe = {}
        dhalf_pe = {}
        gt_ac = {}
        nit = len(iter_list)

        for ib in range(IB):
            buf = ib % 2
            if (1, ib - 2) in gu_pe:
                em.wait("pool", "pe", gu_pe[(1, ib - 2)])
            in_g = gwt.rearrange("(hc p) i -> p hc i", p=128)[:, :, bass.ts(ib, 128)]
            in_u = uwt.rearrange("(hc p) i -> p hc i", p=128)[:, :, bass.ts(ib, 128)]
            fsem = f"gdf{ib % 2}"
            em.op("pool", fsem, lambda e, b=buf, a=in_g: e.dma_start(
                out=gw_st[b], in_=a), 16)
            gu_dma = em.op("pool", fsem, lambda e, b=buf, a=in_u: e.dma_start(
                out=uw_st[b], in_=a), 16)
            gbank = ib % 2
            ubank = 2 + ib % 2
            em.wait("pe", fsem, gu_dma)
            if ib == 0:
                em.wait("pe", "ac", it_exp_ac[nit - 1])
                em.wait("pe", "dv", it_max_dv[nit - 1])
            for th in range(2):
                prev = (1, ib - 2) if th == 0 else (0, ib)
                if th == 1 or ib >= 2:
                    em.wait("pe", "ac", sg_ac.get(prev, 0))
                    em.wait("pe", "dv", hdn_dv.get(prev, 0))
                gmm = umm = 0
                for hc in range(DC):
                    fn = lambda e, b=buf, hc=hc, gb=gbank, th=th: e.matmul(
                        ps_s[:, 512 * gb: 512 * gb + 512],
                        gw_st[b][:, 128 * hc: 128 * (hc + 1)],
                        own_rhs(hc, th), start=(hc == 0), stop=(hc == DC - 1))
                    if hc == DC - 1:
                        gmm = em.op("pe", "pe", fn)
                    else:
                        em.opq("pe", fn)
                for hc in range(DC):
                    fn = lambda e, b=buf, hc=hc, ub=ubank, th=th: e.matmul(
                        ps_s[:, 512 * ub: 512 * ub + 512],
                        uw_st[b][:, 128 * hc: 128 * (hc + 1)],
                        own_rhs(hc, th), start=(hc == 0), stop=(hc == DC - 1))
                    if hc == DC - 1:
                        umm = em.op("pe", "pe", fn)
                    else:
                        em.opq("pe", fn)
                gu_pe[(th, ib)] = umm
                em.wait("act", "pe", gmm)
                em.wait("act", "sdc", consts_sd)
                if prev in hdn_dv and (th == 1 or ib >= 2):
                    em.wait("act", "dv", hdn_dv[prev])
                sgd = em.op("act", "ac", lambda e, b=th, gb=gbank, ib=ib: e.activation(
                    sg_sb[b], ps_s[:, 512 * gb: 512 * gb + 512], AF.Silu,
                    scale=gst_sb[:, ib: ib + 1]))
                sg_ac[(th, ib)] = sgd
                em.wait("dve", "pe", umm)
                em.wait("dve", "ac", sgd)
                hd = em.op("dve", "dv", lambda e, b=th, ub=ubank, ib=ib, th=th:
                           e.scalar_tensor_tensor(
                    hdnT[:, 1024 * ib + 512 * th: 1024 * ib + 512 * th + 512],
                    ps_s[:, 512 * ub: 512 * ub + 512],
                    ust_sb[:, ib: ib + 1], sg_sb[b],
                    op0=ALU.mult, op1=ALU.mult))
                hdn_dv[(th, ib)] = hd

        if "hdn" in tap_aps:
            tap("hdn", hdnT[:, 0: IB * 512], "dv", em.cnt["dv"], IB * 512)

        cvd = Carver(); cvd.off = dw_carve
        dw_st = [cvd.take(22 * 128) for _ in range(2)]
        for hc in range(DC):
            for th in range(2):
                db = (ps_z, ps_w2)[th]
                if hc >= 1:
                    em.wait("pe", "ac", gt_ac[(th, hc - 1)])
                elif th == 0:
                    em.wait("pe", "ac", it_zev_ac[nit - 1])
                    em.wait("pe", "dv", it_rec_dv[nit - 1])
            dmm = {0: 0, 1: 0}
            for hf in range(2):
                hidx = 2 * hc + hf
                buf = hidx % 2
                if hidx - 2 in dhalf_pe:
                    em.wait("pool", "pe", dhalf_pe[hidx - 2])
                elif hc == 0:
                    em.wait("pool", "pe", gu_pe[(1, IB - 1)])
                in_d = dwt.rearrange("(ib p) h -> p ib h", p=128)[
                    :, 22 * hf: 22 * (hf + 1), bass.ts(hc, 128)]
                dsem = f"gdd{hidx % 2}"
                d_dma = em.op("pool", dsem, lambda e, b=buf, a=in_d: e.dma_start(
                    out=dw_st[b], in_=a), 16)
                em.wait("pe", dsem, d_dma)
                if hc == 0 and hf == 0:
                    em.wait("pe", "dv", hdn_dv[(1, IB - 1)])
                for th in range(2):
                    db = (ps_z, ps_w2)[th]
                    for i22 in range(22):
                        ib = 22 * hf + i22
                        fn = lambda e, b=buf, i22=i22, ib=ib, db=db, th=th: e.matmul(
                            db[:, 0:512], dw_st[b][:, 128 * i22: 128 * (i22 + 1)],
                            hdnT[:, 1024 * ib + 512 * th: 1024 * ib + 512 * th + 512],
                            start=(ib == 0), stop=(ib == IB - 1))
                        if i22 == 21:
                            dmm[th] = em.op("pe", "pe", fn)
                        else:
                            em.opq("pe", fn)
                dhalf_pe[hidx] = dmm[1]
            for th in range(2):
                db = (ps_z, ps_w2)[th]
                em.wait("act", "pe", dmm[th])
                gtd = em.op("act", "ac", lambda e, hc=hc, th=th, db=db: e.activation(
                    gT[:, hc * TQ + 512 * th: hc * TQ + 512 * (th + 1)], db[:, 0:512],
                    AF.Silu, scale=dst_sb[:, hc: hc + 1]))
                gt_ac[(th, hc)] = gtd

        if "gt" in tap_aps:
            tap("gt", gT, "ac", em.cnt["ac"], DC * 1024)

        em.barrier()
        cv = Carver()
        wo_st = [cv.take(H * 128) for _ in range(2)]
        sT_sb = [cv.take(512) for _ in range(2)]
        cT_sb = [cv.take(512) for _ in range(2)]
        out_sb = [cv.take(4096, F32) for _ in range(8)]

        op_mm = {}
        op_st = {}
        op_ct = {}
        op_ctp = {}
        last_gt_ac = gt_ac[(1, DC - 1)]
        last_ztev = it_ztev_dv[nit - 1]

        xres_last = 0
        for jj in range(8):
            em.op("sp", "sdr", lambda e, jj=jj: e.dma_start(
                out=out_sb[jj][0:64, :], in_=xk[256 * jj: 256 * jj + 64, :]), 16)
            xres_last = em.op("sp", "sdr", lambda e, jj=jj: e.dma_start(
                out=out_sb[jj][64:128, :], in_=xk[256 * jj + 128: 256 * jj + 192, :]), 16)
        cmb_dv = [0] * 8
        op_pending = [None]
        first_cmb = [True]

        def _emit_ctp(th, dcq):
            em.wait("pe", "dv", op_ct[(th, dcq)])
            par = tp_par[0]
            tp_par[0] ^= 1
            bank = (ps_ta, ps_tb)[par]
            em.wait("pe", "dv", war_tp[par])
            ctp = 0
            for j in range(4):
                fn = lambda e, b=th, j=j, bank=bank: e.transpose(
                    bank[:, 128 * j: 128 * (j + 1)],
                    cT_sb[b][:, 128 * j: 128 * (j + 1)], id_sb[:, :])
                if j == 3:
                    ctp = em.op("pe", "pe", fn)
                else:
                    em.opq("pe", fn)
            op_ctp[(th, dcq)] = ctp
            em.wait("dve", "pe", ctp)
            if first_cmb[0]:
                em.wait("dve", "sdr", xres_last)
                first_cmb[0] = False
            for j in range(4):
                jj = 4 * th + j
                fn = lambda e, j=j, jj=jj, dcq=dcq, bank=bank: e.tensor_tensor(
                    out_sb[jj][:, 128 * dcq: 128 * (dcq + 1)],
                    bank[:, 128 * j: 128 * (j + 1)],
                    out_sb[jj][:, 128 * dcq: 128 * (dcq + 1)], op=ALU.add)
                cmb_dv[jj] = em.op("dve", "dv", fn)
            war_tp[par] = cmb_dv[4 * th + 3]
        for dcq in range(DC):
            buf = dcq % 2
            if dcq >= 2:
                em.wait("pool", "pe", op_mm[(1, dcq - 2)])
            in_w = wo.rearrange("h e d -> e h d")[:, :, bass.ts(dcq, 128)]
            osem = f"gdo{dcq % 2}"
            wod = em.op("pool", osem, lambda e, b=buf, a=in_w: e.dma_start(
                out=wo_st[b], in_=a), 16)
            em.wait("pe", osem, wod)
            em.wait("pe", "dv", last_ztev)
            for th in range(2):
                ob = (ps_z, ps_w2)[th]
                if dcq >= 1:
                    em.wait("pe", "ac", op_st[(th, dcq - 1)])
                else:
                    em.wait("pe", "ac", gt_ac[(th, DC - 1)])
                omm = 0
                for hh in range(H):
                    fn = lambda e, b=buf, hh=hh, ob=ob, th=th: e.matmul(
                        ob[:, 0:512], wo_st[b][:, 128 * hh: 128 * (hh + 1)],
                        zT[:, hh * TQ + 512 * th: hh * TQ + 512 * (th + 1)],
                        start=(hh == 0), stop=(hh == H - 1))
                    if hh == H - 1:
                        omm = em.op("pe", "pe", fn)
                    else:
                        em.opq("pe", fn)
                op_mm[(th, dcq)] = omm
                em.wait("act", "pe", omm)
                if dcq >= 1:
                    em.wait("act", "dv", op_ct.get((th, dcq - 1), 0))
                std = em.op("act", "ac", lambda e, b=th, dcq=dcq, ob=ob: e.activation(
                    sT_sb[b], ob[:, 0:512], AF.Identity,
                    bias=bot_sb[:, dcq: dcq + 1]))
                op_st[(th, dcq)] = std
                em.wait("dve", "ac", std)
                em.wait("dve", "ac", last_gt_ac)
                if (th, dcq - 1) in op_ctp:
                    em.wait("dve", "pe", op_ctp[(th, dcq - 1)])
                ctd = em.op("dve", "dv", lambda e, b=th, dcq=dcq, th=th: e.tensor_tensor(
                    cT_sb[b], sT_sb[b],
                    gT[:, dcq * TQ + 512 * th: dcq * TQ + 512 * (th + 1)], op=ALU.mult))
                op_ct[(th, dcq)] = ctd
                if op_pending[0] is not None:
                    _emit_ctp(*op_pending[0])
                op_pending[0] = (th, dcq)
        if op_pending[0] is not None:
            _emit_ctp(*op_pending[0])
        for jj in range(8):
            em.wait("sp", "dv", cmb_dv[jj])
            em.op("sp", "sdy", lambda e, jj=jj: e.dma_start(
                out=y[bass.ts(jj, 128), :], in_=out_sb[jj][:, :]), 16)

        for s in Emitter.SEMS[:14]:
            em.wait("sp", s, em.cnt[s])
        _emit_blocks(nc, em)
    return nc


def _last(dicts):
    m = 0
    for d in dicts:
        if d:
            m = max(m, max(d.values()))
    return m


def _finalize_partial(em, y, zero):
    em.wait("pool", "dv", em.cnt["dv"])
    em.wait("pool", "ac", em.cnt["ac"])
    em.wait("pool", "pe", em.cnt["pe"])
    z = em.op("pool", "po", lambda e, zz=zero: e.memset(zz, 0.0))
    em.wait("sp", "po", z)
    for j in range(8):
        em.op("sp", "sdy", lambda e, j=j, zz=zero: e.dma_start(
            out=y[bass.ts(j, 128), :], in_=zz), 16)
    for s in Emitter.SEMS[:14]:
        em.wait("sp", s, em.cnt[s])


def _emit_blocks(nc, em):
    with nc.Block() as block:
        @block.tensor
        def _(t):
            for fn in em.s["pe"]:
                fn(t)

        @block.vector
        def _(v):
            for fn in em.s["dve"]:
                fn(v)

        @block.scalar
        def _(s):
            for fn in em.s["act"]:
                fn(s)

        @block.gpsimd
        def _(g):
            for fn in em.s["pool"]:
                fn(g)

        @block.sync
        def _(s):
            for fn in em.s["sp"]:
                fn(s)


def _host_inputs(full, core):
    import ml_dtypes
    b = core >> 1
    rho = core & 1
    d = {}
    xb = np.asarray(full["L0_attn_x"][b], np.float32)
    if rho:
        xb = xb[np.arange(NK) ^ 64]
    d["xk"] = np.ascontiguousarray(xb)
    d["wq"] = np.ascontiguousarray(np.asarray(full["L0_attn_W_Q"], np.float32))
    d["wk"] = np.ascontiguousarray(np.asarray(full["L0_attn_W_K"], np.float32))
    d["wv"] = np.ascontiguousarray(np.asarray(full["L0_attn_W_V"], np.float32))
    d["wo"] = np.ascontiguousarray(np.asarray(full["L0_attn_W_O"], np.float32))
    d["bqt"] = np.ascontiguousarray(np.asarray(full["L0_attn_b_Q"], np.float32).T)
    d["bkt"] = np.ascontiguousarray(np.asarray(full["L0_attn_b_K"], np.float32).T)
    d["bvrow"] = np.ascontiguousarray(np.asarray(full["L0_attn_b_V"], np.float32).reshape(1, H * E))
    d["bot"] = np.ascontiguousarray(np.asarray(full["L0_attn_b_O"], np.float32).reshape(DC, 128).T)
    d["gwt"] = np.ascontiguousarray(np.asarray(full["L0_ffn_gate_w"], np.float32).T)
    d["uwt"] = np.ascontiguousarray(np.asarray(full["L0_ffn_up_w"], np.float32).T)
    d["dwt"] = np.ascontiguousarray(np.asarray(full["L0_ffn_down_w"], np.float32).T)
    d["gst"] = np.ascontiguousarray(np.asarray(full["L0_ffn_gate_s"], np.float32).reshape(IB, 128).T)
    d["ust"] = np.ascontiguousarray(np.asarray(full["L0_ffn_up_s"], np.float32).reshape(IB, 128).T)
    d["dst"] = np.ascontiguousarray(np.asarray(full["L0_ffn_down_s"], np.float32).reshape(DC, 128).T)
    d["ident"] = np.eye(128, dtype=ml_dtypes.bfloat16)
    u = np.arange(128)[:, None]
    jj = np.arange(128)[None, :]
    qoff = np.where(u < 64, 64 * rho + u, 128 + 64 * rho + (u - 64)).astype(np.int64)
    koff = (jj ^ (64 * rho)).astype(np.int64)
    m1 = np.where(koff > qoff, NEG_BIG, 0.0).astype(np.float32)
    m2 = np.where(128 + koff > qoff, NEG_BIG, 0.0).astype(np.float32)
    mk = np.concatenate([m1, m2], axis=1)
    d["maskp"] = np.ascontiguousarray(mk)
    d["mask01"] = np.ascontiguousarray((mk == 0.0).astype(ml_dtypes.bfloat16))
    return d


def _perm(rho):
    r = np.arange(TQ)
    return 64 * (2 * (r // 64) + rho) + (r % 64)


_NC_CACHE = {}


def _get_nc(taps=(), upto="b3"):
    key = (tuple(taps), upto)
    if key not in _NC_CACHE:
        _NC_CACHE[key] = build_nc(taps, upto)
    return _NC_CACHE[key]


def kernel(**inputs):
    nc = _get_nc()
    in_maps = [_host_inputs(inputs, c) for c in range(8)]
    res = run_bass_kernel_spmd(nc, in_maps, core_ids=list(range(8)))
    out = np.empty((4, NK, D), np.float32)
    for c in range(8):
        b, rho = c >> 1, c & 1
        out[b, _perm(rho)] = res.results[c]["y"]
    return out
